# version 08
# baseline (speedup 1.0000x reference)
import numpy as np
import ml_dtypes
from contextlib import ExitStack
import concourse.bass as bass
import concourse.mybir as mybir
from concourse.bass_utils import run_bass_kernel_spmd

F32 = mybir.dt.float32
BF16 = mybir.dt.bfloat16
AF = mybir.ActivationFunctionType
ALU = mybir.AluOpType
AX = mybir.AxisListType
f32 = np.float32

DEBUG = False
PHASE_LIMIT = 99

T = 4096
D = 1024
J = 8
C = 8
NPAIR = 8


class Prog:
    ENGS = ("t", "s", "v", "g", "y")

    def __init__(self, nc, ctx):
        self.nc = nc
        self.q = {k: [] for k in self.ENGS}
        self.sems = {}
        self.cnt = {}
        for n in ("sp", "sa", "sv", "dg", "ds", "pl", "dr", "dy"):
            self.sems[n] = ctx.enter_context(nc.semaphore(n))
            self.cnt[n] = 0

    def op(self, eng, fn, waits=(), inc=None, k=1):
        tok = None
        if inc is not None:
            self.cnt[inc] += k
            tok = self.cnt[inc]
        sems = self.sems
        wl = [(sems[s], v) for (s, v) in waits if v and v > 0]
        incs = (sems[inc], k) if inc is not None else None

        def run(e):
            for (h, v) in wl:
                e.wait_ge(h, v)
            ins = fn(e)
            if incs is not None:
                ins.then_inc(incs[0], incs[1])
        self.q[eng].append(run)
        return tok

    def block(self):
        nc = self.nc
        q = self.q
        with nc.Block() as blk:
            if q["g"]:
                @blk.gpsimd
                def _(e):
                    for fn in q["g"]:
                        fn(e)
            if q["t"]:
                @blk.tensor
                def _(e):
                    for fn in q["t"]:
                        fn(e)
            if q["s"]:
                @blk.scalar
                def _(e):
                    for fn in q["s"]:
                        fn(e)
            if q["v"]:
                @blk.vector
                def _(e):
                    for fn in q["v"]:
                        fn(e)
            if q["y"]:
                @blk.sync
                def _(e):
                    for fn in q["y"]:
                        fn(e)
        self.q = {k: [] for k in self.ENGS}


def build():
    nc = bass.Bass("TRN2", target_bir_lowering=False)

    xin = nc.dram_tensor("xin", [D, T], F32, kind="ExternalInput")
    w0qkvT = nc.dram_tensor("w0qkvT", [D, D], F32, kind="ExternalInput")
    w0outT = nc.dram_tensor("w0outT", [D, D], F32, kind="ExternalInput")
    tempd = nc.dram_tensor("tempd", [16, 1], F32, kind="ExternalInput")
    boutd = nc.dram_tensor("boutd", [128, 8], F32, kind="ExternalInput")
    th0 = nc.dram_tensor("th0", [D, D], F32, kind="ExternalInput")
    thn0 = nc.dram_tensor("thn0", [D, D], F32, kind="ExternalInput")
    ph0 = nc.dram_tensor("ph0", [D, D], F32, kind="ExternalInput")
    phn0 = nc.dram_tensor("phn0", [D, D], F32, kind="ExternalInput")
    w1qkv = nc.dram_tensor("w1qkv", [D, 3 * D], F32, kind="ExternalInput")
    w1out = nc.dram_tensor("w1out", [D, D], BF16, kind="ExternalInput")
    cbT = nc.dram_tensor("cbT", [D, D], F32, kind="ExternalInput")
    scbd = nc.dram_tensor("scbd", [128, 8], F32, kind="ExternalInput")
    bias1d = nc.dram_tensor("bias1d", [128, 8], F32, kind="ExternalInput")
    cosvd = nc.dram_tensor("cosvd", [128, T], BF16, kind="ExternalInput")
    sinvd = nc.dram_tensor("sinvd", [128, T], BF16, kind="ExternalInput")
    permd = nc.dram_tensor("permd", [128, 128], F32, kind="ExternalInput")
    identd = nc.dram_tensor("identd", [128, 128], F32, kind="ExternalInput")
    maskd = nc.dram_tensor("maskd", [128, 128], F32, kind="ExternalInput")
    seld = nc.dram_tensor("seld", [16, D], F32, kind="ExternalInput")

    ikind = "ExternalOutput"
    x1s = nc.dram_tensor("x1s", [D, T], F32, kind=ikind)
    aos = nc.dram_tensor("aos", [D, T], BF16, kind=ikind)
    xout = nc.dram_tensor("xout", [D, T], F32, kind="ExternalOutput")
    dbg = {}
    if DEBUG:
        for nm, shape, dt in (("dxn", [D, T], BF16), ("dw", [D, T], BF16),
                              ("dgg", [D, T], BF16), ("dpi", [16, T], BF16),
                              ("dna", [128, 8], F32), ("dxn1", [D, T], BF16),
                              ("dqp", [128, T], BF16), ("dkp", [128, T], BF16),
                              ("dvt", [128, 6144], BF16), ("dcw", [128, 8192], BF16),
                              ("dsw", [128, 8192], BF16), ("dlg", [16, T], BF16),
                              ("dptm", [128, 512], BF16), ("dnsq", [128, 64], F32),
                              ("dinv", [128, 128], BF16)):
            dbg[nm] = nc.dram_tensor(nm, shape, dt, kind="ExternalOutput")

    rr = lambda d: d.rearrange("(c p) t -> p c t", p=128)
    xin_r, x1s_r, xout_r, aos_r = rr(xin), rr(x1s), rr(xout), rr(aos)

    with ExitStack() as ctx:
        sbt = lambda name, shape, dt: ctx.enter_context(nc.sbuf_tensor(name, shape, dt))

        xng = sbt("xng", [128, 32768], BF16)
        wbuf = sbt("wbuf", [128, 32768], BF16)
        warA = sbt("warA", [128, 8192], BF16)
        warB = sbt("warB", [128, 8192], BF16)
        sgx = sbt("sgx", [128, 8192], BF16)
        xbf = sbt("xbf", [128, 4096], BF16)
        pbuf = sbt("pbuf", [128, 4096], BF16)
        identb = sbt("identb", [128, 128], BF16)
        permb = sbt("permb", [128, 128], BF16)
        maskb = sbt("maskb", [128, 128], BF16)
        selb = sbt("selb", [16, 1024], BF16)
        temp_s = sbt("temp_s", [16, 1], F32)
        bout_s = sbt("bout_s", [128, 8], F32)
        scb_s = sbt("scb_s", [128, 8], F32)
        bias1_s = sbt("bias1_s", [128, 8], F32)
        ones_bf = sbt("ones_bf", [128, 1], BF16)
        onesr_bf = sbt("onesr_bf", [1, 128], BF16)
        rows_f = sbt("rows_f", [1, 1024], F32)
        rows_b = sbt("rows_b", [1, 1024], BF16)
        nsqpart = sbt("nsqpart", [128, 64], F32)
        inv2b = sbt("inv2b", [128, 8], BF16)
        selinv = sbt("selinv", [128, 128], BF16)
        dotpart = sbt("dotpart", [128, 64], F32)
        negattn = sbt("negattn", [128, 8], F32)
        pnbc = sbt("pnbc", [128, 8], F32)
        pn_f = sbt("pn_f", [16, 1], F32)
        pn_b = sbt("pn_b", [16, 1], BF16)
        Pi_tm = sbt("Pi_tm", [128, 512], BF16)
        negmax = sbt("negmax", [128, 1], F32)
        ssum = sbt("ssum", [128, 1], F32)
        sinvv = sbt("sinvv", [128, 1], F32)
        t128a = sbt("t128a", [128, 1], F32)
        t128b = sbt("t128b", [128, 1], F32)
        rbcsb = sbt("rbcsb", [64, 512], F32)
        m32 = sbt("m32", [128, 32], F32)
        d8 = sbt("d8", [128, 8], F32)
        ps = ctx.enter_context(nc.psum_tensor("ps", [128, 4096], F32))

        hm16 = xbf
        sg_f = sgx[:, 0:8192].bitcast(F32)
        sc1 = pbuf[:, 0:1024].bitcast(F32)
        sc2 = pbuf[:, 1024:2048].bitcast(F32)
        wsq0 = pbuf[:, 2048:2560]
        wsq1 = pbuf[:, 2560:3072]
        junkf = pbuf[:, 3072:4096].bitcast(F32)
        ln_sc = wbuf[:, 0:1024].bitcast(F32)
        lnq0 = wbuf[:, 1024:1536]
        lnq1 = wbuf[:, 1536:2048]

        def bank(b, w=512):
            return ps[:, 512 * b:512 * b + w]

        P = Prog(nc, ctx)
        op = P.op
        cnt = P.cnt

        def g_relay(sem_from, tok):
            def f(e):
                e.wait_ge(P.sems[sem_from], tok)
                return e.sem_inc(P.sems["dr" if sem_from == "dg" else "dy"], 1)
            return f

        def vspacer(inc=None):
            return op("v", lambda e: e.tensor_scalar(
                pbuf[:, 1024:2048].bitcast(F32), pbuf[:, 1024:2048].bitcast(F32),
                1.0, 0.0, ALU.mult, ALU.add), inc=inc)
        PI2 = f32(np.pi / 2.0)

        def xg(c, j, w=512, base=128 * 0):
            return xng[:, 4096 * c + 512 * j:4096 * c + 512 * j + w]

        def wg(c, j, w=512):
            return wbuf[:, 4096 * c + 512 * j:4096 * c + 512 * j + w]

        op("y", lambda e: e.dma_start(temp_s[:, :], tempd[:, :]), inc="ds", k=16)
        op("y", lambda e: e.dma_start(bout_s[:, :], boutd[:, :]), inc="ds", k=16)
        op("y", lambda e: e.dma_start(scb_s[:, :], scbd[:, :]), inc="ds", k=16)
        op("y", lambda e: e.dma_start(bias1_s[:, :], bias1d[:, :]), inc="ds", k=16)
        tok_sconsts = cnt["ds"]
        op("g", lambda e: e.dma_start(identb[:, :], identd[:, :]), inc="dg", k=16)
        op("g", lambda e: e.dma_start(permb[:, :], permd[:, :]), inc="dg", k=16)
        op("g", lambda e: e.dma_start(maskb[:, :], maskd[:, :]), inc="dg", k=16)
        op("g", lambda e: e.dma_start(selb[:, :], seld[:, :]), inc="dg", k=16)
        tok_gconsts = cnt["dg"]
        tok_x1init = op("y", lambda e: e.dma_start(x1s[:, :], xin[:, :]), inc="ds", k=16)
        op("g", lambda e: e.memset(ones_bf[:, :], 1.0))
        op("g", lambda e: e.memset(onesr_bf[:, :], 1.0))
        tok_selinv0 = op("g", lambda e: e.memset(selinv[:, :], 0.0), inc="pl")
        op("v", lambda e: e.tensor_scalar(scb_s[:, :], scb_s[:, :], 1.0 / 127.0, 0.0,
                                          ALU.mult, ALU.add),
           waits=[("ds", tok_sconsts)], inc="sv")

        sin_toks = []
        for si, (src, nsrc, dst, bias) in enumerate(
                ((th0, thn0, warA, PI2), (ph0, phn0, warB, f32(0.0)))):
            for a in range(C):
                sl = (si * C + a) % 2
                A = sg_f[:, 2048 * sl:2048 * sl + 1024]
                B = sg_f[:, 2048 * sl + 1024:2048 * sl + 2048]
                w = []
                if len(sin_toks) >= 2:
                    w = [sin_toks[-2]]
                dt1 = op("g", (lambda A=A, src=src, a=a: lambda e: e.dma_start(
                    A, src.rearrange("(a p) o -> p a o", p=128)[:, a, :]))(),
                    waits=w, inc="dg", k=16)
                dt2 = op("g", (lambda B=B, nsrc=nsrc, a=a: lambda e: e.dma_start(
                    B, nsrc.rearrange("(a p) o -> p a o", p=128)[:, a, :]))(),
                    inc="dg", k=16)
                op("g", g_relay("dg", dt2))
                cnt["dr"] += 1
                rlt = cnt["dr"]
                vt = op("v", (lambda A=A, B=B: lambda e: e.tensor_tensor(A, A, B, ALU.add))(),
                        waits=[("dr", rlt)], inc="sv")
                if si == 0:
                    st = op("s", (lambda A=A, B=B: lambda e: e.activation(
                        B, A, AF.Sin, bias=PI2, scale=f32(1.0)))(),
                        waits=[("sv", vt)], inc="sa")
                    ts2 = op("v", (lambda dst=dst, a=a, B=B: lambda e: e.tensor_scalar(
                        dst[:, a * 1024:(a + 1) * 1024], B, 1.0, -1.0,
                        ALU.mult, ALU.add))(), waits=[("sa", st)], inc="sv")
                    sin_toks.append(("sv", ts2))
                else:
                    st = op("s", (lambda dst=dst, a=a, A=A, bias=bias: lambda e: e.activation(
                        dst[:, a * 1024:(a + 1) * 1024], A, AF.Sin, bias=bias,
                        scale=f32(1.0)))(), waits=[("sv", vt)], inc="sa")
                    sin_toks.append(("sa", st))
        tok_w0qkv = op("g", lambda e: e.dma_start(
            sgx[:, :].rearrange("p (a o) -> p a o", a=C),
            w0qkvT.rearrange("(a p) o -> p a o", p=128)), inc="dg", k=16)
        op("g", lambda e: e.wait_ge(P.sems["dg"], cnt["dg"]))
        op("y", lambda e: e.wait_ge(P.sems["ds"], tok_sconsts))
        P.block()

        def ln_phase(src_r, extra=None, jwaits=None):
            if extra:
                extra()
            xn_done = [0] * J
            for j in range(J):
                buf = xbf if j % 2 == 0 else pbuf
                w = [("sv", xn_done[j - 2])] if j >= 2 else []
                if jwaits:
                    w = w + jwaits(j)
                dt0 = op("g", (lambda buf=buf, j=j: lambda e: e.dma_start(
                    buf[:, :].rearrange("p (c t) -> p c t", c=C),
                    src_r[:, :, 512 * j:512 * j + 512]))(), waits=w, inc="dg", k=16)
                op("g", g_relay("dg", dt0))
                cnt["dr"] += 1
                dt = ("dr", cnt["dr"])
                squse = [0] * C
                for c in range(C):
                    dst = lnq0 if c % 2 == 0 else lnq1
                    w = [dt] if c == 0 else []
                    if c >= 2:
                        w.append(("sp", squse[c - 2]))
                    sq = op("s", (lambda dst=dst, buf=buf, c=c: lambda e: e.activation(
                        dst, buf[:, 512 * c:512 * c + 512], AF.Square))(),
                        waits=w, inc="sa")
                    w2 = [dt] if c == 0 else []
                    op("t", (lambda buf=buf, c=c: lambda e: e.matmul(
                        ps[0:1, 4 * 512:4 * 512 + 512], ones_bf[:, :],
                        buf[:, 512 * c:512 * c + 512], start=(c == 0), stop=(c == C - 1)))(),
                        waits=w2)
                    squse[c] = op("t", (lambda dst=dst, c=c: lambda e: e.matmul(
                        ps[0:1, 5 * 512:5 * 512 + 512], ones_bf[:, :], dst,
                        start=(c == 0), stop=(c == C - 1)))(),
                        waits=[("sa", sq)], inc="sp")
                w = [("sp", squse[C - 1])]
                if j > 0:
                    w.append(("sp", bc_last[0]))
                op("v", lambda e: e.tensor_scalar(rows_f[0:1, 0:512],
                                                  ps[0:1, 4 * 512:4 * 512 + 512],
                                                  -1.0 / 1024.0, 0.0, ALU.mult, ALU.add),
                   waits=w)
                op("v", lambda e: e.scalar_tensor_tensor(
                    rows_f[0:1, 512:1024], rows_f[0:1, 0:512], 1.0,
                    rows_f[0:1, 0:512], ALU.mult, ALU.mult))
                vart = op("v", lambda e: e.scalar_tensor_tensor(
                    rows_f[0:1, 512:1024], ps[0:1, 5 * 512:5 * 512 + 512],
                    1.0 / 1024.0, rows_f[0:1, 512:1024], ALU.mult, ALU.subtract),
                    inc="sv")
                sdt = op("s", lambda e: e.activation(rows_f[0:1, 512:1024],
                                                     rows_f[0:1, 512:1024],
                                                     AF.Sqrt, bias=f32(1e-5)),
                         waits=[("sv", vart)], inc="sa")
                op("v", lambda e: e.reciprocal(rows_f[0:1, 512:1024],
                                               rows_f[0:1, 512:1024]),
                   waits=[("sa", sdt)])
                rbt = op("v", lambda e: e.tensor_copy(rows_b[0:1, 512:1024],
                                                      rows_f[0:1, 512:1024]))
                rbt = op("v", lambda e: e.tensor_copy(rows_b[0:1, 0:512],
                                                      rows_f[0:1, 0:512]), inc="sv")
                wrt = [("sv", rbt)]
                if j > 0:
                    wrt.append(("sv", evt_last[0]))
                rt = op("t", lambda e: e.matmul(bank(2), onesr_bf[:, :],
                                                rows_b[0:1, 512:1024],
                                                start=True, stop=True),
                        waits=wrt, inc="sp")
                evt = op("v", lambda e: e.tensor_copy(ln_sc, bank(2)),
                         waits=[("sp", rt)], inc="sv")
                evt_last[0] = evt
                xnt = [0] * C
                for c in range(C):
                    w = []
                    if c >= 2:
                        w.append(("sv", xnt[c - 2]))
                    elif j > 0:
                        w.append(("sv", xn_done[j - 1]))
                    op("t", (lambda c=c: lambda e: e.matmul(
                        bank(c % 2), onesr_bf[:, :], rows_b[0:1, 0:512],
                        start=True, stop=False))(), waits=w)
                    bt = op("t", (lambda buf=buf, c=c: lambda e: e.matmul(
                        bank(c % 2), identb[:, :], buf[:, 512 * c:512 * c + 512],
                        start=False, stop=True))(), inc="sp")
                    ww = [("sp", bt)]
                    if c == 0:
                        ww.append(("sv", evt))
                    xnt[c] = op("v", (lambda c=c, j=j: lambda e: e.tensor_tensor(
                        xg(c, j), bank(c % 2), ln_sc, ALU.mult))(),
                        waits=ww, inc="sv")
                bc_last[0] = bt
                xn_done[j] = xnt[C - 1]
            P.block()

        bc_last = [0]
        evt_last = [0]

        if DEBUG:
            op("y", lambda e: e.dma_start(dbg["dcw"][:, :], warA[:, :]), inc="ds", k=16)
            op("y", lambda e: e.dma_start(dbg["dsw"][:, :], warB[:, :]), inc="ds", k=16)
            op("y", lambda e: e.wait_ge(P.sems["ds"], cnt["ds"]))
            P.block()
        ln_phase(xin_r)

        def tap(name, src_ap, rearr=True):
            if not DEBUG:
                return
            d = dbg[name]
            if rearr:
                op("y", lambda e: e.dma_start(d.rearrange("(c p) t -> p c t", p=128),
                                              src_ap), inc="ds", k=16)
            else:
                op("y", lambda e: e.dma_start(d[:, :], src_ap), inc="ds", k=16)
            op("y", lambda e: e.wait_ge(P.sems["ds"], cnt["ds"]))
            P.block()

        tap("dxn", xng[:, :].rearrange("p (c t) -> p c t", c=C))
        if PHASE_LIMIT < 2:
            return nc, ctx

        k = 0
        sq2 = {}
        for j in range(J):
            for o in range(C):
                bre, bim = bank(2 * (k % 2)), bank(2 * (k % 2) + 1)
                w = [("sa", sq2.get(k - 2, 0))] if k >= 2 else []
                for a in range(C):
                    op("t", (lambda a=a, o=o, j=j, bre=bre: lambda e: e.matmul(
                        bre, warA[:, a * 1024 + o * 128:a * 1024 + o * 128 + 128],
                        xg(a, j), start=(a == 0), stop=(a == C - 1)))(),
                        waits=(w if a == 0 else []))
                mt = None
                for a in range(C):
                    mt = op("t", (lambda a=a, o=o, j=j, bim=bim: lambda e: e.matmul(
                        bim, warB[:, a * 1024 + o * 128:a * 1024 + o * 128 + 128],
                        xg(a, j), start=(a == 0), stop=(a == C - 1)))(),
                        inc=("sp" if a == C - 1 else None))
                op("s", (lambda bre=bre: lambda e: e.activation(sc1, bre, AF.Square))(),
                   waits=[("sp", mt)])
                s2 = op("s", (lambda bim=bim: lambda e: e.activation(sc2, bim, AF.Square))(),
                        inc="sa")
                sq2[k] = s2
                vt = op("v", lambda e: e.tensor_tensor(sc1, sc1, sc2, ALU.add),
                        waits=[("sa", s2)], inc="sv")
                op("s", (lambda o=o, j=j: lambda e: e.activation(
                    wg(o, j), sc1, AF.Sqrt))(), waits=[("sv", vt)], inc="sa")
                k += 1
        P.block()
        if PHASE_LIMIT < 3:
            return nc, ctx

        evac = {}
        for j in range(J):
            lastmm = None
            for o in range(C):
                w = []
                if j > 0:
                    w += [("sa", evac[(j - 1, o)][0]), ("sv", evac[(j - 1, o)][1])]
                for a in range(C):
                    lastmm = op("t", (lambda a=a, o=o, j=j: lambda e: e.matmul(
                        bank(o), sgx[:, a * 1024 + o * 128:a * 1024 + o * 128 + 128],
                        xg(a, j), start=(a == 0), stop=(a == C - 1)))(),
                        waits=(w if a == 0 else []),
                        inc=("sp" if a == C - 1 else None))
            for o in range(C):
                dst = wsq0 if o % 2 == 0 else wsq1
                wq = [("sp", lastmm)] if o == 0 else []
                if o >= 2:
                    wq.append(("sv", evac[(j, o - 2)][1]))
                at = op("s", (lambda o=o, j=j, dst=dst: lambda e: e.activation(
                    dst, bank(o), AF.Square))(), waits=wq, inc="sa")
                op("v", (lambda o=o, j=j: lambda e: e.tensor_copy(
                    xg(o, j), bank(o)))(), waits=[("sa", at)])
                vt = op("v", (lambda o=o, j=j, dst=dst: lambda e: e.tensor_reduce(
                    nsqpart[:, 8 * o + j:8 * o + j + 1], dst, AX.X, ALU.add))(),
                    inc="sv")
                evac[(j, o)] = (at, vt)
        P.block()
        if PHASE_LIMIT < 4:
            return nc, ctx

        op("v", lambda e: e.tensor_reduce(
            d8[:, :], nsqpart[:, :].rearrange("p (o j) -> p o j", j=8),
            AX.X, ALU.add), waits=[("pl", tok_selinv0)])
        vspacer()
        op("v", lambda e: e.tensor_scalar(d8[:, :], d8[:, :], 1e-24, 0.0,
                                          ALU.max, ALU.add))
        vspacer()
        op("v", lambda e: e.reciprocal(d8[:, :], d8[:, :]))
        vspacer()
        op("v", lambda e: e.tensor_copy(inv2b[:, :], d8[:, :]))
        vspacer()
        for o in range(C):
            op("v", (lambda o=o: lambda e: e.tensor_copy(
                selinv[0:64, 18 * o:18 * o + 1], inv2b[0:64, o:o + 1]))())
            op("v", (lambda o=o: lambda e: e.tensor_copy(
                selinv[64:128, 18 * o + 1:18 * o + 2], inv2b[64:128, o:o + 1]))())
        vspacer(inc="sv")
        P.block()
        if PHASE_LIMIT < 5:
            return nc, ctx

        for h in range(16):
            op("s", (lambda h=h: lambda e: e.activation(
                wbuf[:, 2048 * h:2048 * h + 2048], wbuf[:, 2048 * h:2048 * h + 2048],
                AF.Silu))(), inc="sa")
        P.block()
        tap("dw", xng[:, :].rearrange("p (c t) -> p c t", c=C))
        tap("dgg", wbuf[:, :].rearrange("p (c t) -> p c t", c=C))
        if PHASE_LIMIT < 6:
            return nc, ctx

        tok_w0out = op("g", lambda e: e.dma_start(
            warA[:, :].rearrange("p (a o) -> p a o", a=C),
            w0outT.rearrange("(a p) o -> p a o", p=128)), inc="dg", k=16)
        tok_cbt = op("g", lambda e: e.dma_start(
            warB[:, :].rearrange("p (a o) -> p a o", a=C),
            cbT.rearrange("(a p) o -> p a o", p=128)), inc="dg", k=16)
        mmt = {}
        evt_j = {}
        for j in range(J):
            for c in range(C):
                src = wsq0 if c % 2 == 0 else wsq1
                w = [("sp", mmt.get((j, c - 2), mmt.get((j - 1, C - 1), 0)))] \
                    if (c >= 2 or j > 0) else []
                vt = op("v", (lambda src=src, c=c, j=j: lambda e: e.tensor_tensor(
                    src, xg(c, j), xg(c, j), ALU.mult))(), waits=w, inc="sv")
                wm = [("sv", vt)]
                if c == 0 and j > 0:
                    wm.append(("sv", evt_j[j - 1]))
                mmt[(j, c)] = op("t", (lambda src=src, c=c: lambda e: e.matmul(
                    ps[0:16, 0:512], selinv[:, 16 * c:16 * c + 16], src,
                    start=(c == 0), stop=(c == C - 1)))(),
                    waits=wm, inc="sp")
            evt_j[j] = op("v", (lambda j=j: lambda e: e.tensor_scalar(
                hm16[0:16, 512 * j:512 * j + 512], ps[0:16, 0:512],
                temp_s[:, :], 0.0, ALU.mult, ALU.add))(),
                waits=[("sp", mmt[(j, C - 1)])], inc="sv")
        P.block()
        tap("dnsq", nsqpart[:, :], rearr=False)
        tap("dinv", selinv[:, :], rearr=False)
        tap("dlg", hm16[0:16, :], rearr=False)
        if PHASE_LIMIT < 7:
            return nc, ctx

        lgT = Pi_tm
        cp_t = {}
        for k in range(32):
            bA = ps[:, (0 + k % 2) * 512:(0 + k % 2) * 512 + 512].bitcast(BF16)[:, 0:16]
            w = [("sv", cp_t.get(k - 2, 0))]
            tp1 = op("t", (lambda bA=bA, k=k: lambda e: e.transpose(
                bA, hm16[0:16, 128 * k:128 * k + 128], identb[0:16, 0:16]))(),
                waits=w, inc="sp")
            cp_t[k] = op("v", (lambda bA=bA, k=k: lambda e: e.tensor_copy(
                lgT[:, 16 * k:16 * k + 16], bA))(), waits=[("sp", tp1)], inc="sv")
        vspacer()
        lg3 = lgT[:, :].rearrange("p (k h) -> p k h", h=16)
        mxbc = bass.AP(m32, 0, [[32, 128], [1, 32], [0, 16]])
        op("v", lambda e: e.tensor_reduce(m32[:, :], lg3, AX.X, ALU.max))
        vspacer()
        svt = op("v", lambda e: e.tensor_tensor(lg3, lg3, mxbc, ALU.subtract),
                 inc="sv")
        ext = op("s", lambda e: e.activation(lgT[:, :], lgT[:, :], AF.Exp),
                 waits=[("sv", svt)], inc="sa")
        op("v", lambda e: e.tensor_reduce(m32[:, :], lg3, AX.X, ALU.add),
           waits=[("sa", ext)])
        vspacer()
        op("v", lambda e: e.reciprocal(m32[:, :], m32[:, :]))
        vspacer()
        pit = op("v", lambda e: e.tensor_tensor(lg3, lg3, mxbc, ALU.mult),
                 inc="sv")
        hm_t = {}
        for k in range(32):
            bB = ps[:, (2 + k % 2) * 512:(2 + k % 2) * 512 + 512].bitcast(BF16)[0:16, 0:128]
            w = [("sv", pit)] if k == 0 else []
            if k >= 2:
                w.append(("sv", hm_t[k - 2]))
            op("t", (lambda k=k: lambda e: e.matmul(
                ps[0:16, 5 * 512:5 * 512 + 1], Pi_tm[:, 16 * k:16 * k + 16],
                ones_bf[:, :], start=(k == 0), stop=(k == 31)))(), waits=w)
            tp2 = op("t", (lambda bB=bB, k=k: lambda e: e.transpose(
                bB, Pi_tm[:, 16 * k:16 * k + 16], identb[:, :]))(), inc="sp")
            hm_t[k] = op("v", (lambda bB=bB, k=k: lambda e: e.tensor_copy(
                hm16[0:16, 128 * k:128 * k + 128], bB))(),
                waits=[("sp", tp2)], inc="sv")
        last_pisum_tok = cnt["sp"]
        op("v", lambda e: e.tensor_copy(pn_f[:, :], ps[0:16, 5 * 512:5 * 512 + 1]),
           waits=[("sp", last_pisum_tok)])
        vspacer()
        op("v", lambda e: e.tensor_scalar(pn_f[:, :], pn_f[:, :], 1e-8, 0.0,
                                          ALU.add, ALU.add))
        vspacer()
        op("v", lambda e: e.reciprocal(pn_f[:, :], pn_f[:, :]))
        vspacer()
        pnt = op("v", lambda e: e.tensor_copy(pn_b[:, :], pn_f[:, :]), inc="sv")
        vspacer(inc="sv")
        for c in range(C):
            bt = op("t", (lambda c=c: lambda e: e.matmul(
                ps[:, 4 * 512 + 64 * c:4 * 512 + 64 * c + 1],
                selb[:, 128 * c:128 * c + 128], pn_b[:, :], start=True, stop=True))(),
                waits=[("sv", cnt["sv"])] if c == 0 else [], inc="sp")
            op("v", (lambda c=c: lambda e: e.tensor_copy(
                pnbc[:, c:c + 1], ps[:, 4 * 512 + 64 * c:4 * 512 + 64 * c + 1]))(),
                waits=[("sp", bt)])
        vspacer(inc="sv")
        P.block()
        tap("dpi", hm16[0:16, :], rearr=False)
        tap("dptm", Pi_tm[:, :], rearr=False)
        if PHASE_LIMIT < 8:
            return nc, ctx

        k = 0
        stt_t = {}
        for c in range(C):
            for j in range(J):
                w = [("sv", stt_t.get(k - 2, 0))] if k >= 2 else []
                bt = op("t", (lambda c=c, j=j, k=k: lambda e: e.matmul(
                    bank(k % 2), selb[:, 128 * c:128 * c + 128],
                    hm16[0:16, 512 * j:512 * j + 512], start=True, stop=True))(),
                    waits=w, inc="sp")
                op("v", (lambda c=c, j=j: lambda e: e.tensor_tensor(
                    wsq0, xg(c, j), xg(c, j), ALU.mult))())
                op("v", (lambda c=c, j=j, k=k: lambda e: e.scalar_tensor_tensor(
                    junkf, wsq0, 1.0, bank(k % 2), ALU.mult, ALU.mult))(),
                    waits=[("sp", bt)])
                stt_t[k] = op("v", (lambda c=c, j=j: lambda e: e.tensor_reduce(
                    dotpart[:, 8 * c + j:8 * c + j + 1], junkf, AX.X, ALU.add))(),
                    inc="sv")
                k += 1
        P.block()
        op("v", lambda e: e.tensor_reduce(
            d8[:, :], dotpart[:, :].rearrange("p (c j) -> p c j", j=8),
            AX.X, ALU.add))
        vspacer()
        op("v", lambda e: e.tensor_tensor(d8[:, :], d8[:, :], pnbc[:, :], ALU.mult))
        vspacer()
        op("v", lambda e: e.tensor_scalar(d8[:, :], d8[:, :], 1.0, 1.0,
                                          ALU.mult, ALU.add))
        vspacer()
        op("v", lambda e: e.reciprocal(d8[:, :], d8[:, :]))
        vspacer()
        op("v", lambda e: e.tensor_scalar(negattn[:, :], d8[:, :], -1.0, 0.0,
                                          ALU.mult, ALU.add))
        vspacer(inc="sv")
        P.block()
        tap("dna", negattn[:, :], rearr=False)
        if PHASE_LIMIT < 9:
            return nc, ctx

        op("g", g_relay("dg", tok_cbt))
        cnt["dr"] += 1
        tok_R3 = cnt["dr"]
        oe_t = {}
        sg_t = {}
        acc_t = {}
        for j in range(J):
            for c in range(C):
                w = []
                if c >= 2:
                    w.append(("sv", oe_t[(j, c - 2)]))
                elif j > 0:
                    w.append(("sv", oe_t[(j - 1, C - 2 + c)]))
                if j > 0:
                    w.append(("sp", sproj_last[0]))
                bt = op("t", (lambda c=c, j=j: lambda e: e.matmul(
                    bank(c % 2), selb[:, 128 * c:128 * c + 128],
                    hm16[0:16, 512 * j:512 * j + 512], start=True, stop=True))(),
                    waits=w, inc="sp")
                oe_t[(j, c)] = op("v", (lambda c=c, j=j: lambda e: e.scalar_tensor_tensor(
                    pbuf[:, 512 * c:512 * c + 512], xg(c, j), negattn[:, c:c + 1],
                    bank(c % 2), ALU.mult, ALU.mult))(),
                    waits=[("sp", bt)], inc="sv")
            lastmm = None
            for o in range(C):
                w = [("sv", oe_t[(j, C - 1)])] if o == 0 else []
                if o == 0 and j == 0:
                    w.append(("dr", tok_R3))
                if o >= 2:
                    w.append(("sv", sg_t[(j, o - 2)]))
                elif j > 0:
                    w.append(("sv", sg_t[(j - 1, C - 2 + o)]))
                for a in range(C):
                    lastmm = op("t", (lambda a=a, o=o: lambda e: e.matmul(
                        bank(2 + o % 2), warA[:, a * 1024 + o * 128:a * 1024 + o * 128 + 128],
                        pbuf[:, 512 * a:512 * a + 512],
                        start=(a == 0), stop=(a == C - 1)))(),
                        waits=(w if a == 0 else []),
                        inc=("sp" if a == C - 1 else None))
                ww = [("sp", lastmm)]
                if j >= 1:
                    ww.append(("dg", acc_t[j - 1]))
                sg_t[(j, o)] = op("v", (lambda o=o, j=j: lambda e: e.scalar_tensor_tensor(
                    sg_f[:, 512 * o:512 * o + 512], bank(2 + o % 2), bout_s[:, o:o + 1],
                    wg(o, j), ALU.add, ALU.mult))(), waits=ww, inc="sv")
            sproj_last = [lastmm]
            w = [("sv", sg_t[(j, C - 1)])]
            if j == 0:
                w.append(("ds", tok_x1init))
            acc_t[j] = op("g", (lambda j=j: lambda e: e.dma_start(
                x1s_r[:, :, 512 * j:512 * j + 512],
                sg_f[:, :].rearrange("p (c t) -> p c t", c=C), accum_op=ALU.add))(),
                waits=w, inc="dg", k=16)
        tok_x1done = acc_t[J - 1]
        P.block()
        if PHASE_LIMIT < 10:
            return nc, ctx

        tok_xout_init = [0]
        tok_tables = [0]
        tok_w1out = [0]

        def b7_extra():
            op("y", lambda e: e.dma_start(sgx[:, 0:4096], cosvd[:, :]), inc="ds", k=16)
            tok_tables[0] = op("y", lambda e: e.dma_start(sgx[:, 4096:8192],
                                                          sinvd[:, :]), inc="ds", k=16)
            tok_w1out[0] = op("y", lambda e: e.dma_start(
                warA[:, :].rearrange("p (a o) -> p a o", a=C),
                w1out.rearrange("(a p) o -> p a o", p=128)), inc="ds", k=16)
        b6_mm_done = sproj_last[0]
        ln_phase(x1s_r, extra=b7_extra,
                 jwaits=lambda j: [("dg", acc_t[j]), ("sp", b6_mm_done)])
        tok_xout_init[0] = op("y", lambda e: e.dma_start(xout[:, :], x1s[:, :]),
                              waits=[("dg", tok_x1done)], inc="ds", k=16)
        tap("dxn1", xng[:, :].rearrange("p (c t) -> p c t", c=C))
        if PHASE_LIMIT < 11:
            return nc, ctx

        QSB, KSB, QP, KP = 0, 4096, 8192, 12288
        VTO, AOF, RSC = 16384, 22528, 26624
        qsb = wbuf[:, QSB:QSB + 4096]
        ksb = wbuf[:, KSB:KSB + 4096]
        Qp_t = wbuf[:, QP:QP + 4096]
        Kp_t = wbuf[:, KP:KP + 4096]
        aof = wbuf[:, AOF:AOF + 4096]
        rsc1 = wbuf[:, RSC:RSC + 1024].bitcast(F32)
        rsc2 = wbuf[:, RSC + 1024:RSC + 2048].bitcast(F32)
        rbcF = wbuf[:, 28672:29696].bitcast(F32)
        rsc3 = wbuf[:, 29696:30720].bitcast(F32)
        rsc4 = wbuf[:, 30720:31744].bitcast(F32)
        cosv_sb = sgx[:, 0:4096]
        sinv_sb = sgx[:, 4096:8192]
        wsl = xbf
        pA = pbuf[:, 0:2048]
        pB = pbuf[:, 2048:4096]

        def vto_ap(tt):
            return bass.AP(wbuf, VTO + 192 * tt, [[32768, 128], [128, 2], [1, 64]])

        def vto_sl(tt, h):
            lo = VTO + 192 * tt + 64 * h
            return wbuf[:, lo:lo + 128]

        aof_dma_tok = [0]
        tok_vto_ones = [0]
        tok_R7 = [0]
        prev_wsl_free = [0]
        prev_cpy7 = [0]
        cpy_tok = {}
        for p in range(NPAIR):
            if p == 0:
                op("g", g_relay("dg", cnt["dg"]))
                cnt["dr"] += 1
                tok_R7[0] = cnt["dr"]
                tok_vto_ones[0] = op("g", lambda e: e.memset(
                    wbuf[:, VTO:VTO + 6144], 1.0), inc="pl")
            dws = None
            for qi, off in ((0, 128 * p), (1, D + 128 * p), (2, 2 * D + 128 * p)):
                wd = [("sv", prev_wsl_free[0])] if (qi == 0 and p > 0) else []
                dws = op("g", (lambda qi=qi, off=off: lambda e: e.dma_start(
                    wsl[:, 1024 * qi:1024 * qi + 1024].rearrange("p (a o) -> p a o", a=C),
                    w1qkv.rearrange("(a p) d -> p a d", p=128)[:, :, off:off + 128]))(),
                    waits=wd, inc="dg", k=16)
            ev_t = {}
            for qi in range(2):
                dst = qsb if qi == 0 else ksb
                for j in range(J):
                    kk = qi * J + j
                    w = [("dg", dws)] if kk == 0 else []
                    if kk >= 2:
                        w.append(("sv", ev_t[kk - 2]))
                    lm = None
                    for a in range(C):
                        lm = op("t", (lambda qi=qi, a=a, j=j, kk=kk: lambda e: e.matmul(
                            bank(kk % 2), wsl[:, 1024 * qi + a * 128:1024 * qi + a * 128 + 128],
                            xg(a, j), start=(a == 0), stop=(a == C - 1)))(),
                            waits=(w if a == 0 else []),
                            inc=("sp" if a == C - 1 else None))
                    ev_t[kk] = op("v", (lambda dst=dst, j=j, kk=kk: lambda e: e.tensor_copy(
                        dst[:, 512 * j:512 * j + 512], bank(kk % 2)))(),
                        waits=[("sp", lm)], inc="sv")
            rot_t = {}
            for qi in range(2):
                src = qsb if qi == 0 else ksb
                dst = Qp_t if qi == 0 else Kp_t
                for j in range(J):
                    kk = qi * J + j
                    w = [("sv", ev_t[kk])]
                    if p == 0 and kk == 0:
                        w.append(("ds", tok_tables[0]))
                    if kk >= 2:
                        w.append(("sv", rot_t[kk - 2]))
                    pm = op("t", (lambda src=src, j=j, kk=kk: lambda e: e.matmul(
                        bank(4 + kk % 2), permb[:, :], src[:, 512 * j:512 * j + 512],
                        start=True, stop=True))(), waits=w, inc="sp")
                    op("v", (lambda src=src, j=j: lambda e: e.tensor_tensor(
                        rsc1, src[:, 512 * j:512 * j + 512],
                        cosv_sb[:, 512 * j:512 * j + 512], ALU.mult))(),
                        waits=[("sv", ev_t[kk])])
                    op("v", (lambda j=j, kk=kk: lambda e: e.tensor_tensor(
                        rsc2, bank(4 + kk % 2), sinv_sb[:, 512 * j:512 * j + 512],
                        ALU.mult))(), waits=[("sp", pm)])
                    rot_t[kk] = op("v", (lambda dst=dst, j=j: lambda e: e.tensor_tensor(
                        dst[:, 512 * j:512 * j + 512], rsc1, rsc2, ALU.add))(),
                        inc="sv")

            vv_t = {}
            vfm_cast = {}
            bank_rd = {4: rot_t[15], 5: rot_t[15]}

            def emit_vfm(j):
                b = 4 + j % 2
                w = [("sv", bank_rd[b])] if bank_rd[b] else []
                if j == 0:
                    w.append(("sv", rot_t[15]))
                lm = None
                for a in range(C):
                    lm = op("t", (lambda a=a, j=j, b=b: lambda e: e.matmul(
                        bank(b), wsl[:, 2048 + a * 128:2048 + a * 128 + 128],
                        xg(a, j), start=(a == 0), stop=(a == C - 1)))(),
                        waits=(w if a == 0 else []),
                        inc=("sp" if a == C - 1 else None))
                wv = [("sp", lm)]
                if j == 0:
                    wv.append(("sv", rot_t[7]))
                vfm_cast[j] = op("v", (lambda j=j, b=b: lambda e: e.tensor_copy(
                    qsb[:, 512 * j:512 * j + 512], bank(b)))(), waits=wv, inc="sv")
                bank_rd[b] = vfm_cast[j]
                if j == 7:
                    prev_wsl_free[0] = vfm_cast[7]

            def emit_tp(tt):
                b = 4 + tt % 2
                bA = ps[:, 512 * b:512 * b + 512].bitcast(BF16)[:, 0:128]
                w = [("sv", vfm_cast[tt // 4])]
                if bank_rd[b]:
                    w.append(("sv", bank_rd[b]))
                tpk = op("t", (lambda bA=bA, tt=tt: lambda e: e.transpose(
                    bA, qsb[:, 128 * tt:128 * tt + 128], identb[:, :]))(),
                    waits=w, inc="sp")
                wv = [("sp", tpk)]
                if p == 0 and tt == 0:
                    wv.append(("pl", tok_vto_ones[0]))
                vv_t[tt] = op("v", (lambda bA=bA, tt=tt: lambda e: e.tensor_copy(
                    vto_ap(tt), bA))(), waits=wv, inc="sv")
                bank_rd[b] = vv_t[tt]

            for j in range(8):
                emit_vfm(j)
                for K in range(4 * j, 4 * j + 4):
                    emit_tp(K)
            if DEBUG and p == 0:
                op("y", lambda e: e.dma_start(dbg["dqp"][:, :], Qp_t[:, :]), inc="ds", k=16)
                op("y", lambda e: e.dma_start(dbg["dkp"][:, :], Kp_t[:, :]), inc="ds", k=16)
                op("y", lambda e: e.dma_start(dbg["dvt"][:, :], wbuf[:, VTO:VTO + 6144]),
                   inc="ds", k=16)
                op("y", lambda e: e.wait_ge(P.sems["ds"], cnt["ds"]))
                P.block()

            units = [(qt, kc) for qt in range(8) for kc in range(4 * qt + 4)]
            NU = len(units)
            exp_u = {}
            pv_u = {}
            evac_q = {}

            def PQ(ui):
                return pbuf[:, 1024 * (ui % 4):1024 * (ui % 4) + 1024]

            def emit_qk(ui):
                qt, kc = units[ui]
                c0 = max(0, kc * 128 - qt * 512)
                diag = 0 <= kc * 128 - qt * 512 < 512
                sbb = 2 * (ui % 2)
                w = []
                if ui >= 2:
                    w.append(("sa", exp_u[ui - 2]))
                qk_tok = None
                for h in range(2):
                    ww = w if h == 0 else []
                    stop = not diag
                    qk_tok = op("t", (lambda h=h, c0=c0, kc=kc, qt=qt, sbb=sbb, stop=stop:
                                      lambda e: e.matmul(
                        ps[:, 512 * sbb + 512 * h + c0:512 * sbb + 512 * h + 512],
                        Kp_t[64 * h:64 * h + 64, 128 * kc:128 * kc + 128],
                        Qp_t[64 * h:64 * h + 64, 512 * qt + c0:512 * qt + 512],
                        start=True, stop=stop, tile_position=(64 * h, 0)))(),
                        waits=ww,
                        inc=("sp" if (h == 1 and not diag) else None))
                if diag:
                    for h in range(2):
                        qk_tok = op("t", (lambda h=h, c0=c0, sbb=sbb: lambda e: e.matmul(
                            ps[:, 512 * sbb + 512 * h + c0:512 * sbb + 512 * h + c0 + 128],
                            identb[:, :], maskb[:, :], start=False, stop=True))(),
                            inc=("sp" if h == 1 else None))
                return qk_tok

            pend_evac = [None]
            qk_toks = {0: emit_qk(0), 1: emit_qk(1)}
            for ui, (qt, kc) in enumerate(units):
                c0 = max(0, kc * 128 - qt * 512)
                sbb = 2 * (ui % 2)
                pbb = PQ(ui)
                aob = 6
                first = kc == 0
                last = kc == 4 * qt + 3
                w = [("sp", qk_toks[ui])]
                if ui >= 4:
                    w.append(("sp", pv_u[ui - 4]))
                if c0 == 0:
                    exp_u[ui] = op("s", (lambda sbb=sbb, pbb=pbb: lambda e: e.activation(
                        pbb[:, 0:1024], ps[:, 512 * sbb:512 * sbb + 1024], AF.Exp,
                        bias=f32(-8.0), scale=f32(0.125)))(), waits=w, inc="sa")
                else:
                    op("s", (lambda sbb=sbb, pbb=pbb, c0=c0: lambda e: e.activation(
                        pbb[:, c0:512], ps[:, 512 * sbb + c0:512 * sbb + 512], AF.Exp,
                        bias=f32(-8.0), scale=f32(0.125)))(), waits=w)
                    exp_u[ui] = op("s", (lambda sbb=sbb, pbb=pbb, c0=c0: lambda e: e.activation(
                        pbb[:, 512 + c0:1024], ps[:, 512 * sbb + 512 + c0:512 * sbb + 1024],
                        AF.Exp, bias=f32(-8.0), scale=f32(0.125)))(), inc="sa")
                for h in range(2):
                    w = [("sa", exp_u[ui])] if h == 0 else []
                    if h == 0 and qt == kc // 4:
                        w.append(("sv", vv_t[kc]))
                    if h == 0 and first:
                        if qt >= 1:
                            w.append(("sv", cpy_tok[qt - 1]))
                        elif p > 0:
                            w.append(("sv", prev_cpy7[0]))
                    pv_u[ui] = op("t", (lambda h=h, kc=kc, qt=qt, c0=c0, pbb=pbb, aob=aob,
                                        first=first, last=last: lambda e: e.matmul(
                        ps[:, 512 * (aob + h) + c0:512 * (aob + h) + 512],
                        vto_sl(kc, h), pbb[:, 512 * h + c0:512 * h + 512],
                        start=first, stop=last))(),
                        waits=w, inc=("sp" if h == 1 else None))
                if ui + 2 < NU:
                    qk_toks[ui + 2] = emit_qk(ui + 2)
                if last:
                    rscF = (rsc1, rsc2) if qt % 2 == 0 else (rsc3, rsc4)
                    cps = []
                    for h in range(2):
                        wch = [("sp", pv_u[ui])] if h == 0 else []
                        cps.append(op("v", (lambda h=h, rd=rscF[h]: lambda e: e.tensor_copy(
                            rd, ps[:, 512 * (6 + h):512 * (6 + h) + 512]))(),
                            waits=wch, inc="sv"))
                    cpy_tok[qt] = cps[1]
                    if qt == 7:
                        prev_cpy7[0] = cps[1]

                    def flush_evac(fq, frsc, fcps):
                        et = None
                        for h in range(2):
                            aorow = 0 if h == 0 else 64
                            denrow = 64 if h == 0 else 0
                            op("v", (lambda h=h, rd=frsc[h], denrow=denrow, aorow=aorow:
                                     lambda e: e.reciprocal(
                                rbcF[aorow:aorow + 64, :], rd[denrow:denrow + 64, :]))(),
                                waits=[("sv", fcps[h])])
                            w3 = []
                            if p > 0 and fq == 0 and h == 0:
                                w3.append(("ds", aof_dma_tok[0]))
                            et = op("v", (lambda h=h, fq=fq, rd=frsc[h], aorow=aorow:
                                          lambda e: e.tensor_tensor(
                                aof[64 * h:64 * h + 64, 512 * fq:512 * fq + 512],
                                rd[aorow:aorow + 64, :],
                                rbcF[aorow:aorow + 64, :], ALU.mult))(),
                                waits=w3, inc="sv")
                        evac_q[fq] = et

                    if pend_evac[0] is not None:
                        flush_evac(*pend_evac[0])
                    pend_evac[0] = (qt, rscF, cps)
                    if qt == 7:
                        flush_evac(*pend_evac[0])
                        pend_evac[0] = None
            aof_dma_tok[0] = op("y", (lambda p=p: lambda e: e.dma_start(
                aos[128 * p:128 * p + 128, :], aof[:, :]))(),
                waits=[("sv", evac_q[7])], inc="ds", k=16)
        P.block()
        if PHASE_LIMIT < 12:
            return nc, ctx

        fsc = xbf[:, 0:1024].bitcast(F32)
        g1_t = {}
        ffn_vt = {}
        sg1_t = {}
        acc1_t = {}
        ldj = {}
        sproj_done = {}
        for j in range(J):
            w = [] if j == 0 else [("sp", sproj_done[j - 1])]
            ld0 = op("y", (lambda j=j: lambda e: e.dma_start(
                pbuf[:, :].rearrange("p (c t) -> p c t", c=C),
                aos_r[:, :, 512 * j:512 * j + 512]))(), waits=w, inc="ds", k=16)
            op("y", g_relay("ds", ld0))
            cnt["dy"] += 1
            ldj[j] = cnt["dy"]
            for o in range(C):
                w = []
                if j == 0 and o == 0:
                    w += [("dr", tok_R3), ("dy", ldj[0] if 0 in ldj else 1)]
                if o >= 2:
                    w.append(("sv", ffn_vt[(j, o - 2)]))
                elif j > 0:
                    w.append(("sv", ffn_vt[(j - 1, C - 2 + o)]))
                lm = None
                for a in range(C):
                    lm = op("t", (lambda a=a, o=o, j=j: lambda e: e.matmul(
                        bank(o % 2), warB[:, a * 1024 + o * 128:a * 1024 + o * 128 + 128],
                        xg(a, j), start=(a == 0), stop=(a == C - 1)))(),
                        waits=(w if a == 0 else []),
                        inc=("sp" if a == C - 1 else None))
                wv = [("sp", lm)]
                prev = (j, o - 1) if o > 0 else ((j - 1, C - 1) if j > 0 else None)
                if prev is not None:
                    wv.append(("sa", g1_t[prev]))
                ffn_vt[(j, o)] = op("v", (lambda o=o: lambda e: e.tensor_scalar(
                    fsc, bank(o % 2), scb_s[:, o:o + 1], bias1_s[:, o:o + 1],
                    ALU.mult, ALU.add))(), waits=wv, inc="sv")
                g1_t[(j, o)] = op("s", (lambda o=o, j=j: lambda e: e.activation(
                    wg(o, j), fsc, AF.Silu))(),
                    waits=[("sv", ffn_vt[(j, o)])], inc="sa")
            lastmm = None
            for o in range(C):
                w = [("dy", ldj[j])] if o == 0 else []
                if o == 0 and j == 0:
                    w.append(("ds", tok_w1out[0]))
                if o >= 2:
                    w.append(("sv", sg1_t[(j, o - 2)]))
                elif j > 0:
                    w.append(("sv", sg1_t[(j - 1, C - 2 + o)]))
                for a in range(C):
                    lastmm = op("t", (lambda a=a, o=o: lambda e: e.matmul(
                        bank(2 + o % 2), warA[:, a * 1024 + o * 128:a * 1024 + o * 128 + 128],
                        pbuf[:, 512 * a:512 * a + 512],
                        start=(a == 0), stop=(a == C - 1)))(),
                        waits=(w if a == 0 else []),
                        inc=("sp" if a == C - 1 else None))
                ww = [("sp", lastmm), ("sa", g1_t[(j, o)])]
                if j >= 1:
                    ww.append(("dg", acc1_t[j - 1]))
                sg1_t[(j, o)] = op("v", (lambda o=o, j=j: lambda e: e.scalar_tensor_tensor(
                    sg_f[:, 512 * o:512 * o + 512], bank(2 + o % 2), 1.0,
                    wg(o, j), ALU.mult, ALU.mult))(), waits=ww, inc="sv")
            sproj_done[j] = lastmm
            w = [("sv", sg1_t[(j, C - 1)])]
            if j == 0:
                w.append(("ds", tok_xout_init[0]))
            acc1_t[j] = op("g", (lambda j=j: lambda e: e.dma_start(
                xout_r[:, :, 512 * j:512 * j + 512],
                sg_f[:, :].rearrange("p (c t) -> p c t", c=C), accum_op=ALU.add))(),
                waits=w, inc="dg", k=16)
        op("g", lambda e: e.wait_ge(P.sems["dg"], cnt["dg"]))
        P.block()

    return nc, None


def _host_prep(inputs):
    L0x = np.asarray(inputs["L0_attn_x"], dtype=np.float32)
    B = L0x.shape[0]
    pos = np.arange(T, dtype=np.float32)[:, None]
    ii = np.arange(0, 64, 2, dtype=np.float32)[None, :]
    theta = (pos / np.power(f32(10000.0), ii / f32(64.0))).astype(np.float32)
    cost = np.cos(theta).astype(np.float32)
    sint = np.sin(theta).astype(np.float32)
    dd = (np.arange(128) % 64) % 32
    cosv = np.ascontiguousarray(cost[:, dd].T)
    sinv = np.ascontiguousarray(sint[:, dd].T)
    perm = np.zeros((128, 128), dtype=np.float32)
    for jj in range(128):
        base = (jj // 64) * 64
        d = jj % 64
        if d < 32:
            perm[base + d + 32, jj] = -1.0
        else:
            perm[base + d - 32, jj] = 1.0
    ident = np.eye(128, dtype=np.float32)
    mask = np.where(np.arange(128)[:, None] > np.arange(128)[None, :],
                    np.float32(-1e9), np.float32(0.0)).astype(np.float32)
    sel = np.zeros((16, D), dtype=np.float32)
    for ff in range(D):
        sel[2 * (ff // 128) + (ff % 128) // 64, ff] = 1.0

    common = {
        "w0qkvT": np.ascontiguousarray(np.asarray(inputs["L0_attn_Wqkv"], np.float32).T),
        "w0outT": np.ascontiguousarray(np.asarray(inputs["L0_attn_Wout"], np.float32).T),
        "tempd": np.ascontiguousarray(np.asarray(inputs["L0_attn_temp"], np.float32)),
        "boutd": np.ascontiguousarray(
            np.asarray(inputs["L0_attn_bout"], np.float32).reshape(8, 128).T),
        "th0": np.ascontiguousarray(np.asarray(inputs["L0_ffn_theta"], np.float32)),
        "thn0": np.ascontiguousarray(np.asarray(inputs["L0_ffn_theta_noise"], np.float32)),
        "ph0": np.ascontiguousarray(np.asarray(inputs["L0_ffn_phi"], np.float32)),
        "phn0": np.ascontiguousarray(np.asarray(inputs["L0_ffn_phi_noise"], np.float32)),
        "w1qkv": np.ascontiguousarray(np.asarray(inputs["L1_attn_w_qkv"], np.float32)),
        "w1out": np.ascontiguousarray(np.asarray(inputs["L1_attn_w_out"], np.float32).astype(ml_dtypes.bfloat16)),
        "cbT": np.ascontiguousarray(np.asarray(inputs["L1_ffn_CB"], np.float32).T),
        "scbd": np.ascontiguousarray(
            np.asarray(inputs["L1_ffn_SCB"], np.float32).reshape(8, 128).T),
        "bias1d": np.ascontiguousarray(
            np.asarray(inputs["L1_ffn_bias"], np.float32).reshape(8, 128).T),
        "cosvd": cosv.astype(ml_dtypes.bfloat16), "sinvd": sinv.astype(ml_dtypes.bfloat16), "permd": perm, "identd": ident,
        "maskd": mask, "seld": sel,
    }
    in_maps = []
    for b in range(B):
        m = dict(common)
        m["xin"] = np.ascontiguousarray(L0x[b].T)
        in_maps.append(m)
    return in_maps


def kernel(**inputs):
    in_maps = _host_prep(inputs)
    nc, _ = build()
    res = run_bass_kernel_spmd(nc, in_maps, core_ids=list(range(8)))
    out = np.stack([np.ascontiguousarray(r["xout"].T) for r in res.results], axis=0)
    return out.astype(np.float32)


# version 13 (final)
# speedup vs baseline: 1.1229x; 1.1229x over previous
import numpy as np
import ml_dtypes
from contextlib import ExitStack
import concourse.bass as bass
import concourse.mybir as mybir
from concourse.bass_utils import run_bass_kernel_spmd

F32 = mybir.dt.float32
BF16 = mybir.dt.bfloat16
AF = mybir.ActivationFunctionType
ALU = mybir.AluOpType
AX = mybir.AxisListType
f32 = np.float32

DEBUG = False
PHASE_LIMIT = 99

T = 4096
D = 1024
J = 8
C = 8
NPAIR = 8


class Prog:
    ENGS = ("t", "s", "v", "g", "y")

    def __init__(self, nc, ctx):
        self.nc = nc
        self.q = {k: [] for k in self.ENGS}
        self.sems = {}
        self.cnt = {}
        for n in ("sp", "sa", "sv", "dg", "ds", "pl", "dr", "dy"):
            self.sems[n] = ctx.enter_context(nc.semaphore(n))
            self.cnt[n] = 0

    def op(self, eng, fn, waits=(), inc=None, k=1):
        tok = None
        if inc is not None:
            self.cnt[inc] += k
            tok = self.cnt[inc]
        sems = self.sems
        wl = [(sems[s], v) for (s, v) in waits if v and v > 0]
        incs = (sems[inc], k) if inc is not None else None

        def run(e):
            for (h, v) in wl:
                e.wait_ge(h, v)
            ins = fn(e)
            if incs is not None:
                ins.then_inc(incs[0], incs[1])
        self.q[eng].append(run)
        return tok

    def block(self):
        nc = self.nc
        q = self.q
        with nc.Block() as blk:
            if q["g"]:
                @blk.gpsimd
                def _(e):
                    for fn in q["g"]:
                        fn(e)
            if q["t"]:
                @blk.tensor
                def _(e):
                    for fn in q["t"]:
                        fn(e)
            if q["s"]:
                @blk.scalar
                def _(e):
                    for fn in q["s"]:
                        fn(e)
            if q["v"]:
                @blk.vector
                def _(e):
                    for fn in q["v"]:
                        fn(e)
            if q["y"]:
                @blk.sync
                def _(e):
                    for fn in q["y"]:
                        fn(e)
        self.q = {k: [] for k in self.ENGS}


def build():
    nc = bass.Bass("TRN2", target_bir_lowering=False)

    xin = nc.dram_tensor("xin", [D, T], F32, kind="ExternalInput")
    w0qkvT = nc.dram_tensor("w0qkvT", [D, D], BF16, kind="ExternalInput")
    w0outT = nc.dram_tensor("w0outT", [D, D], F32, kind="ExternalInput")
    tempd = nc.dram_tensor("tempd", [16, 1], F32, kind="ExternalInput")
    boutd = nc.dram_tensor("boutd", [128, 8], F32, kind="ExternalInput")
    th0 = nc.dram_tensor("th0", [D, D], F32, kind="ExternalInput")
    thn0 = nc.dram_tensor("thn0", [D, D], F32, kind="ExternalInput")
    ph0 = nc.dram_tensor("ph0", [D, D], F32, kind="ExternalInput")
    phn0 = nc.dram_tensor("phn0", [D, D], F32, kind="ExternalInput")
    w1qkv = nc.dram_tensor("w1qkv", [D, 3 * D], F32, kind="ExternalInput")
    w1out = nc.dram_tensor("w1out", [D, D], BF16, kind="ExternalInput")
    cbT = nc.dram_tensor("cbT", [D, D], F32, kind="ExternalInput")
    scbd = nc.dram_tensor("scbd", [128, 8], F32, kind="ExternalInput")
    bias1d = nc.dram_tensor("bias1d", [128, 8], F32, kind="ExternalInput")
    cosvd = nc.dram_tensor("cosvd", [128, T], BF16, kind="ExternalInput")
    sinvd = nc.dram_tensor("sinvd", [128, T], BF16, kind="ExternalInput")
    permd = nc.dram_tensor("permd", [128, 128], F32, kind="ExternalInput")
    identd = nc.dram_tensor("identd", [128, 128], F32, kind="ExternalInput")
    maskd = nc.dram_tensor("maskd", [128, 128], F32, kind="ExternalInput")
    seld = nc.dram_tensor("seld", [16, D], F32, kind="ExternalInput")

    ikind = "ExternalOutput"
    x1s = nc.dram_tensor("x1s", [D, T], F32, kind=ikind)
    aos = nc.dram_tensor("aos", [D, T], BF16, kind=ikind)
    xout = nc.dram_tensor("xout", [D, T], F32, kind="ExternalOutput")
    dbg = {}
    if DEBUG:
        for nm, shape, dt in (("dxn", [D, T], BF16), ("dw", [D, T], BF16),
                              ("dgg", [D, T], BF16), ("dpi", [16, T], BF16),
                              ("dna", [128, 8], F32), ("dxn1", [D, T], BF16),
                              ("dqp", [128, T], BF16), ("dkp", [128, T], BF16),
                              ("dvt", [128, 6144], BF16), ("dcw", [128, 8192], BF16),
                              ("dsw", [128, 8192], BF16), ("dlg", [16, T], BF16),
                              ("dptm", [128, 512], BF16), ("dnsq", [128, 64], F32),
                              ("dinv", [128, 128], BF16)):
            dbg[nm] = nc.dram_tensor(nm, shape, dt, kind="ExternalOutput")

    rr = lambda d: d.rearrange("(c p) t -> p c t", p=128)
    xin_r, x1s_r, xout_r, aos_r = rr(xin), rr(x1s), rr(xout), rr(aos)

    with ExitStack() as ctx:
        sbt = lambda name, shape, dt: ctx.enter_context(nc.sbuf_tensor(name, shape, dt))

        xng = sbt("xng", [128, 32768], BF16)
        wbuf = sbt("wbuf", [128, 32768], BF16)
        warA = sbt("warA", [128, 8192], BF16)
        warB = sbt("warB", [128, 8192], BF16)
        sgx = sbt("sgx", [128, 8192], BF16)
        xbf = sbt("xbf", [128, 4096], BF16)
        pbuf = sbt("pbuf", [128, 4096], BF16)
        identb = sbt("identb", [128, 128], BF16)
        permb = sbt("permb", [128, 128], BF16)
        maskb = sbt("maskb", [128, 128], BF16)
        selb = sbt("selb", [16, 1024], BF16)
        temp_s = sbt("temp_s", [16, 1], F32)
        bout_s = sbt("bout_s", [128, 8], F32)
        scb_s = sbt("scb_s", [128, 8], F32)
        bias1_s = sbt("bias1_s", [128, 8], F32)
        ones_bf = sbt("ones_bf", [128, 1], BF16)
        onesr_bf = sbt("onesr_bf", [1, 128], BF16)
        rows_f = sbt("rows_f", [1, 1024], F32)
        rows_b = sbt("rows_b", [1, 1024], BF16)
        nsqpart = sbt("nsqpart", [128, 64], F32)
        inv2b = sbt("inv2b", [128, 8], BF16)
        selinv = sbt("selinv", [128, 128], BF16)
        dotpart = sbt("dotpart", [128, 64], F32)
        negattn = sbt("negattn", [128, 8], F32)
        pnbc = sbt("pnbc", [128, 8], F32)
        pn_f = sbt("pn_f", [16, 1], F32)
        pn_b = sbt("pn_b", [16, 1], BF16)
        Pi_tm = sbt("Pi_tm", [128, 512], BF16)
        negmax = sbt("negmax", [128, 1], F32)
        ssum = sbt("ssum", [128, 1], F32)
        sinvv = sbt("sinvv", [128, 1], F32)
        t128a = sbt("t128a", [128, 1], F32)
        t128b = sbt("t128b", [128, 1], F32)
        rbcsb = sbt("rbcsb", [64, 512], F32)
        m32 = sbt("m32", [128, 32], F32)
        d8 = sbt("d8", [128, 8], F32)
        ps = ctx.enter_context(nc.psum_tensor("ps", [128, 4096], F32))

        hm16 = xbf
        sg_f = sgx[:, 0:8192].bitcast(F32)
        sc1 = pbuf[:, 0:1024].bitcast(F32)
        sc2 = pbuf[:, 1024:2048].bitcast(F32)
        wsq0 = pbuf[:, 2048:2560]
        wsq1 = pbuf[:, 2560:3072]
        junkf = pbuf[:, 3072:4096].bitcast(F32)
        ln_sc = wbuf[:, 0:1024].bitcast(F32)
        lnq0 = wbuf[:, 1024:1536]
        lnq1 = wbuf[:, 1536:2048]

        def bank(b, w=512):
            return ps[:, 512 * b:512 * b + w]

        P = Prog(nc, ctx)
        op = P.op
        cnt = P.cnt

        def g_relay(sem_from, tok):
            def f(e):
                e.wait_ge(P.sems[sem_from], tok)
                return e.sem_inc(P.sems["dr" if sem_from == "dg" else "dy"], 1)
            return f

        def vspacer(inc=None):
            return op("v", lambda e: e.tensor_scalar(
                pbuf[:, 1024:2048].bitcast(F32), pbuf[:, 1024:2048].bitcast(F32),
                1.0, 0.0, ALU.mult, ALU.add), inc=inc)
        PI2 = f32(np.pi / 2.0)

        def xg(c, j, w=512, base=128 * 0):
            return xng[:, 4096 * c + 512 * j:4096 * c + 512 * j + w]

        def wg(c, j, w=512):
            return wbuf[:, 4096 * c + 512 * j:4096 * c + 512 * j + w]

        op("y", lambda e: e.dma_start(temp_s[:, :], tempd[:, :]), inc="ds", k=16)
        op("y", lambda e: e.dma_start(bout_s[:, :], boutd[:, :]), inc="ds", k=16)
        op("y", lambda e: e.dma_start(scb_s[:, :], scbd[:, :]), inc="ds", k=16)
        op("y", lambda e: e.dma_start(bias1_s[:, :], bias1d[:, :]), inc="ds", k=16)
        tok_sconsts = cnt["ds"]
        op("g", lambda e: e.dma_start(identb[:, :], identd[:, :]), inc="dg", k=16)
        op("g", lambda e: e.dma_start(permb[:, :], permd[:, :]), inc="dg", k=16)
        op("g", lambda e: e.dma_start(maskb[:, :], maskd[:, :]), inc="dg", k=16)
        op("g", lambda e: e.dma_start(selb[:, :], seld[:, :]), inc="dg", k=16)
        tok_gconsts = cnt["dg"]
        op("g", lambda e: e.memset(ones_bf[:, :], 1.0))
        op("g", lambda e: e.memset(onesr_bf[:, :], 1.0))
        tok_selinv0 = op("g", lambda e: e.memset(selinv[:, :], 0.0), inc="pl")
        op("v", lambda e: e.tensor_scalar(scb_s[:, :], scb_s[:, :], 1.0 / 127.0, 0.0,
                                          ALU.mult, ALU.add),
           waits=[("ds", tok_sconsts)], inc="sv")

        sin_toks = []
        for si, (src, nsrc, dst, bias) in enumerate(
                ((th0, thn0, warA, PI2), (ph0, phn0, warB, f32(0.0)))):
            for a in range(C):
                sl = (si * C + a) % 2
                A = sg_f[:, 2048 * sl:2048 * sl + 1024]
                B = sg_f[:, 2048 * sl + 1024:2048 * sl + 2048]
                w = []
                if len(sin_toks) >= 2:
                    w = [sin_toks[-2]]
                dt1 = op("y", (lambda A=A, src=src, a=a: lambda e: e.dma_start(
                    A, src.rearrange("(a p) o -> p a o", p=128)[:, a, :]))(),
                    waits=w, inc="ds", k=16)
                dt2 = op("s", (lambda B=B, nsrc=nsrc, a=a: lambda e: e.dma_start(
                    B, nsrc.rearrange("(a p) o -> p a o", p=128)[:, a, :]))(),
                    waits=w, inc="dy", k=16)
                vt = op("v", (lambda A=A, B=B: lambda e: e.tensor_tensor(A, A, B, ALU.add))(),
                        waits=[("ds", dt1), ("dy", dt2)], inc="sv")
                if si == 0:
                    st = op("s", (lambda A=A, B=B: lambda e: e.activation(
                        B, A, AF.Sin, bias=PI2, scale=f32(1.0)))(),
                        waits=[("sv", vt)], inc="sa")
                    ts2 = op("v", (lambda dst=dst, a=a, B=B: lambda e: e.tensor_scalar(
                        dst[:, a * 1024:(a + 1) * 1024], B, 1.0, -1.0,
                        ALU.mult, ALU.add))(), waits=[("sa", st)], inc="sv")
                    sin_toks.append(("sv", ts2))
                else:
                    st = op("s", (lambda dst=dst, a=a, A=A, bias=bias: lambda e: e.activation(
                        dst[:, a * 1024:(a + 1) * 1024], A, AF.Sin, bias=bias,
                        scale=f32(1.0)))(), waits=[("sv", vt)], inc="sa")
                    sin_toks.append(("sa", st))
        tok_w0qkv = op("y", lambda e: e.dma_start(
            sgx[:, :].rearrange("p (a o) -> p a o", a=C),
            w0qkvT.rearrange("(a p) o -> p a o", p=128)),
            waits=[sin_toks[14], sin_toks[15]], inc="ds", k=16)


        def ln_phase(src_r, extra=None, jwaits=None):
            if extra:
                extra()
            xn_done = [0] * J
            for j in range(J):
                buf = xbf if j % 2 == 0 else pbuf
                w = [("sv", xn_done[j - 2])] if j >= 2 else []
                if jwaits:
                    w = w + jwaits(j)
                dt0 = op("g", (lambda buf=buf, j=j: lambda e: e.dma_start(
                    buf[:, :].rearrange("p (c t) -> p c t", c=C),
                    src_r[:, :, 512 * j:512 * j + 512]))(), waits=w, inc="dg", k=16)
                dt = ("dg", dt0)
                squse = [0] * C
                for c in range(C):
                    dst = lnq0 if c % 2 == 0 else lnq1
                    w = [dt] if c == 0 else []
                    if c >= 2:
                        w.append(("sp", squse[c - 2]))
                    sq = op("s", (lambda dst=dst, buf=buf, c=c: lambda e: e.activation(
                        dst, buf[:, 512 * c:512 * c + 512], AF.Square))(),
                        waits=w, inc="sa")
                    w2 = [dt] if c == 0 else []
                    op("t", (lambda buf=buf, c=c: lambda e: e.matmul(
                        ps[0:1, 4 * 512:4 * 512 + 512], ones_bf[:, :],
                        buf[:, 512 * c:512 * c + 512], start=(c == 0), stop=(c == C - 1)))(),
                        waits=w2)
                    squse[c] = op("t", (lambda dst=dst, c=c: lambda e: e.matmul(
                        ps[0:1, 5 * 512:5 * 512 + 512], ones_bf[:, :], dst,
                        start=(c == 0), stop=(c == C - 1)))(),
                        waits=[("sa", sq)], inc="sp")
                w = [("sp", squse[C - 1])]
                if j > 0:
                    w.append(("sp", bc_last[0]))
                op("v", lambda e: e.tensor_scalar(rows_f[0:1, 0:512],
                                                  ps[0:1, 4 * 512:4 * 512 + 512],
                                                  -1.0 / 1024.0, 0.0, ALU.mult, ALU.add),
                   waits=w)
                op("v", lambda e: e.scalar_tensor_tensor(
                    rows_f[0:1, 512:1024], rows_f[0:1, 0:512], 1.0,
                    rows_f[0:1, 0:512], ALU.mult, ALU.mult))
                vart = op("v", lambda e: e.scalar_tensor_tensor(
                    rows_f[0:1, 512:1024], ps[0:1, 5 * 512:5 * 512 + 512],
                    1.0 / 1024.0, rows_f[0:1, 512:1024], ALU.mult, ALU.subtract),
                    inc="sv")
                sdt = op("s", lambda e: e.activation(rows_f[0:1, 512:1024],
                                                     rows_f[0:1, 512:1024],
                                                     AF.Sqrt, bias=f32(1e-5)),
                         waits=[("sv", vart)], inc="sa")
                op("v", lambda e: e.reciprocal(rows_f[0:1, 512:1024],
                                               rows_f[0:1, 512:1024]),
                   waits=[("sa", sdt)])
                rbt = op("v", lambda e: e.tensor_copy(rows_b[0:1, 512:1024],
                                                      rows_f[0:1, 512:1024]))
                rbt = op("v", lambda e: e.tensor_copy(rows_b[0:1, 0:512],
                                                      rows_f[0:1, 0:512]), inc="sv")
                wrt = [("sv", rbt)]
                if j > 0:
                    wrt.append(("sv", evt_last[0]))
                rt = op("t", lambda e: e.matmul(bank(2), onesr_bf[:, :],
                                                rows_b[0:1, 512:1024],
                                                start=True, stop=True),
                        waits=wrt, inc="sp")
                evt = op("v", lambda e: e.tensor_copy(ln_sc, bank(2)),
                         waits=[("sp", rt)], inc="sv")
                evt_last[0] = evt
                xnt = [0] * C
                for c in range(C):
                    w = []
                    if c >= 2:
                        w.append(("sv", xnt[c - 2]))
                    elif j > 0:
                        w.append(("sv", xn_done[j - 1]))
                    op("t", (lambda c=c: lambda e: e.matmul(
                        bank(c % 2), onesr_bf[:, :], rows_b[0:1, 0:512],
                        start=True, stop=False))(), waits=w)
                    bt = op("t", (lambda buf=buf, c=c: lambda e: e.matmul(
                        bank(c % 2), identb[:, :], buf[:, 512 * c:512 * c + 512],
                        start=False, stop=True))(), inc="sp")
                    ww = [("sp", bt)]
                    if c == 0:
                        ww.append(("sv", evt))
                    xnt[c] = op("v", (lambda c=c, j=j: lambda e: e.tensor_tensor(
                        xg(c, j), bank(c % 2), ln_sc, ALU.mult))(),
                        waits=ww, inc="sv")
                bc_last[0] = bt
                xn_done[j] = xnt[C - 1]
            P.block()
            return ("sv", xn_done[J - 1])

        bc_last = [0]
        evt_last = [0]

        if DEBUG:
            op("y", lambda e: e.dma_start(dbg["dcw"][:, :], warA[:, :]), inc="ds", k=16)
            op("y", lambda e: e.dma_start(dbg["dsw"][:, :], warB[:, :]), inc="ds", k=16)
            op("y", lambda e: e.wait_ge(P.sems["ds"], cnt["ds"]))
            P.block()
        ln0_done = ln_phase(xin_r)

        def tap(name, src_ap, rearr=True):
            if not DEBUG:
                return
            d = dbg[name]
            if rearr:
                op("y", lambda e: e.dma_start(d.rearrange("(c p) t -> p c t", p=128),
                                              src_ap), inc="ds", k=16)
            else:
                op("y", lambda e: e.dma_start(d[:, :], src_ap), inc="ds", k=16)
            op("y", lambda e: e.wait_ge(P.sems["ds"], cnt["ds"]))
            P.block()

        tap("dxn", xng[:, :].rearrange("p (c t) -> p c t", c=C))
        if PHASE_LIMIT < 2:
            return nc, ctx

        k = 0
        sq2 = {}
        for j in range(J):
            for o in range(C):
                bre, bim = bank(2 * (k % 2)), bank(2 * (k % 2) + 1)
                w = [("sa", sq2.get(k - 2, 0))] if k >= 2 else (
                    [ln0_done, sin_toks[7], sin_toks[15]] if k == 0 else [])
                for a in range(C):
                    op("t", (lambda a=a, o=o, j=j, bre=bre: lambda e: e.matmul(
                        bre, warA[:, a * 1024 + o * 128:a * 1024 + o * 128 + 128],
                        xg(a, j), start=(a == 0), stop=(a == C - 1)))(),
                        waits=(w if a == 0 else []))
                mt = None
                for a in range(C):
                    mt = op("t", (lambda a=a, o=o, j=j, bim=bim: lambda e: e.matmul(
                        bim, warB[:, a * 1024 + o * 128:a * 1024 + o * 128 + 128],
                        xg(a, j), start=(a == 0), stop=(a == C - 1)))(),
                        inc=("sp" if a == C - 1 else None))
                op("s", (lambda bre=bre: lambda e: e.activation(sc1, bre, AF.Square))(),
                   waits=[("sp", mt)])
                s2 = op("s", (lambda bim=bim: lambda e: e.activation(sc2, bim, AF.Square))(),
                        inc="sa")
                sq2[k] = s2
                vt = op("v", lambda e: e.tensor_tensor(sc1, sc1, sc2, ALU.add),
                        waits=[("sa", s2)], inc="sv")
                op("s", (lambda o=o, j=j: lambda e: e.activation(
                    wg(o, j), sc1, AF.Sqrt))(), waits=[("sv", vt)], inc="sa")
                k += 1
        b2a_last_mm = cnt["sp"]
        if PHASE_LIMIT < 3:
            return nc, ctx

        evac = {}
        for j in range(J):
            lastmm = None
            for o in range(C):
                w = [("ds", tok_w0qkv)] if (j == 0 and o == 0) else []
                if j > 0:
                    w += [("sa", evac[(j - 1, o)][0]), ("sv", evac[(j - 1, o)][1])]
                for a in range(C):
                    lastmm = op("t", (lambda a=a, o=o, j=j: lambda e: e.matmul(
                        bank(o), sgx[:, a * 1024 + o * 128:a * 1024 + o * 128 + 128],
                        xg(a, j), start=(a == 0), stop=(a == C - 1)))(),
                        waits=(w if a == 0 else []),
                        inc=("sp" if a == C - 1 else None))
            for o in range(C):
                dst = wsq0 if o % 2 == 0 else wsq1
                wq = [("sp", lastmm)] if o == 0 else []
                if o >= 2:
                    wq.append(("sv", evac[(j, o - 2)][1]))
                at = op("s", (lambda o=o, j=j, dst=dst: lambda e: e.activation(
                    dst, bank(o), AF.Square))(), waits=wq, inc="sa")
                op("v", (lambda o=o, j=j: lambda e: e.tensor_copy(
                    xg(o, j), bank(o)))(), waits=[("sa", at)])
                vt = op("v", (lambda o=o, j=j, dst=dst: lambda e: e.tensor_reduce(
                    nsqpart[:, 8 * o + j:8 * o + j + 1], dst, AX.X, ALU.add))(),
                    inc="sv")
                evac[(j, o)] = (at, vt)
        if PHASE_LIMIT < 4:
            return nc, ctx

        op("v", lambda e: e.tensor_reduce(
            d8[:, :], nsqpart[:, :].rearrange("p (o j) -> p o j", j=8),
            AX.X, ALU.add), waits=[("pl", tok_selinv0)])
        vspacer()
        op("v", lambda e: e.tensor_scalar(d8[:, :], d8[:, :], 1e-24, 0.0,
                                          ALU.max, ALU.add))
        vspacer()
        op("v", lambda e: e.reciprocal(d8[:, :], d8[:, :]))
        vspacer()
        op("v", lambda e: e.tensor_copy(inv2b[:, :], d8[:, :]))
        vspacer()
        for o in range(C):
            op("v", (lambda o=o: lambda e: e.tensor_copy(
                selinv[0:64, 18 * o:18 * o + 1], inv2b[0:64, o:o + 1]))())
            op("v", (lambda o=o: lambda e: e.tensor_copy(
                selinv[64:128, 18 * o + 1:18 * o + 2], inv2b[64:128, o:o + 1]))())
        vspacer(inc="sv")
        if PHASE_LIMIT < 5:
            return nc, ctx

        for h in range(16):
            op("s", (lambda h=h: lambda e: e.activation(
                wbuf[:, 2048 * h:2048 * h + 2048], wbuf[:, 2048 * h:2048 * h + 2048],
                AF.Silu))(), inc="sa")
        tap("dw", xng[:, :].rearrange("p (c t) -> p c t", c=C))
        tap("dgg", wbuf[:, :].rearrange("p (c t) -> p c t", c=C))
        if PHASE_LIMIT < 6:
            return nc, ctx

        tok_w0out = op("g", lambda e: e.dma_start(
            warA[:, :].rearrange("p (a o) -> p a o", a=C),
            w0outT.rearrange("(a p) o -> p a o", p=128)),
            waits=[("sp", b2a_last_mm)], inc="dg", k=16)
        tok_cbt = op("g", lambda e: e.dma_start(
            warB[:, :].rearrange("p (a o) -> p a o", a=C),
            cbT.rearrange("(a p) o -> p a o", p=128)), inc="dg", k=16)
        mmt = {}
        evt_j = {}
        for j in range(J):
            for c in range(C):
                src = wsq0 if c % 2 == 0 else wsq1
                w = [("sp", mmt.get((j, c - 2), mmt.get((j - 1, C - 1), 0)))] \
                    if (c >= 2 or j > 0) else []
                vt = op("v", (lambda src=src, c=c, j=j: lambda e: e.tensor_tensor(
                    src, xg(c, j), xg(c, j), ALU.mult))(), waits=w, inc="sv")
                wm = [("sv", vt)]
                if c == 0 and j > 0:
                    wm.append(("sv", evt_j[j - 1]))
                mmt[(j, c)] = op("t", (lambda src=src, c=c: lambda e: e.matmul(
                    ps[0:16, 0:512], selinv[:, 16 * c:16 * c + 16], src,
                    start=(c == 0), stop=(c == C - 1)))(),
                    waits=wm, inc="sp")
            evt_j[j] = op("v", (lambda j=j: lambda e: e.tensor_scalar(
                hm16[0:16, 512 * j:512 * j + 512], ps[0:16, 0:512],
                temp_s[:, :], 0.0, ALU.mult, ALU.add))(),
                waits=[("sp", mmt[(j, C - 1)])], inc="sv")
        tap("dnsq", nsqpart[:, :], rearr=False)
        tap("dinv", selinv[:, :], rearr=False)
        tap("dlg", hm16[0:16, :], rearr=False)
        if PHASE_LIMIT < 7:
            return nc, ctx

        lgT = Pi_tm
        cp_t = {}
        for k in range(32):
            bA = ps[:, (0 + k % 2) * 512:(0 + k % 2) * 512 + 512].bitcast(BF16)[:, 0:16]
            w = [("sv", cp_t.get(k - 2, evt_j[J - 1] if k < 2 else 0))]
            tp1 = op("t", (lambda bA=bA, k=k: lambda e: e.transpose(
                bA, hm16[0:16, 128 * k:128 * k + 128], identb[0:16, 0:16]))(),
                waits=w, inc="sp")
            cp_t[k] = op("v", (lambda bA=bA, k=k: lambda e: e.tensor_copy(
                lgT[:, 16 * k:16 * k + 16], bA))(), waits=[("sp", tp1)], inc="sv")
        vspacer()
        lg3 = lgT[:, :].rearrange("p (k h) -> p k h", h=16)
        mxbc = bass.AP(m32, 0, [[32, 128], [1, 32], [0, 16]])
        op("v", lambda e: e.tensor_reduce(m32[:, :], lg3, AX.X, ALU.max))
        vspacer()
        svt = op("v", lambda e: e.tensor_tensor(lg3, lg3, mxbc, ALU.subtract),
                 inc="sv")
        ext = op("s", lambda e: e.activation(lgT[:, :], lgT[:, :], AF.Exp),
                 waits=[("sv", svt)], inc="sa")
        op("v", lambda e: e.tensor_reduce(m32[:, :], lg3, AX.X, ALU.add),
           waits=[("sa", ext)])
        vspacer()
        op("v", lambda e: e.reciprocal(m32[:, :], m32[:, :]))
        vspacer()
        pit = op("v", lambda e: e.tensor_tensor(lg3, lg3, mxbc, ALU.mult),
                 inc="sv")
        hm_t = {}
        for k in range(32):
            bB = ps[:, (2 + k % 2) * 512:(2 + k % 2) * 512 + 512].bitcast(BF16)[0:16, 0:128]
            w = [("sv", pit)] if k == 0 else []
            if k >= 2:
                w.append(("sv", hm_t[k - 2]))
            op("t", (lambda k=k: lambda e: e.matmul(
                ps[0:16, 5 * 512:5 * 512 + 1], Pi_tm[:, 16 * k:16 * k + 16],
                ones_bf[:, :], start=(k == 0), stop=(k == 31)))(), waits=w)
            tp2 = op("t", (lambda bB=bB, k=k: lambda e: e.transpose(
                bB, Pi_tm[:, 16 * k:16 * k + 16], identb[:, :]))(), inc="sp")
            hm_t[k] = op("v", (lambda bB=bB, k=k: lambda e: e.tensor_copy(
                hm16[0:16, 128 * k:128 * k + 128], bB))(),
                waits=[("sp", tp2)], inc="sv")
        last_pisum_tok = cnt["sp"]
        op("v", lambda e: e.tensor_copy(pn_f[:, :], ps[0:16, 5 * 512:5 * 512 + 1]),
           waits=[("sp", last_pisum_tok)])
        vspacer()
        op("v", lambda e: e.tensor_scalar(pn_f[:, :], pn_f[:, :], 1e-8, 0.0,
                                          ALU.add, ALU.add))
        vspacer()
        op("v", lambda e: e.reciprocal(pn_f[:, :], pn_f[:, :]))
        vspacer()
        pnt = op("v", lambda e: e.tensor_copy(pn_b[:, :], pn_f[:, :]), inc="sv")
        vspacer(inc="sv")
        for c in range(C):
            bt = op("t", (lambda c=c: lambda e: e.matmul(
                ps[:, 4 * 512 + 64 * c:4 * 512 + 64 * c + 1],
                selb[:, 128 * c:128 * c + 128], pn_b[:, :], start=True, stop=True))(),
                waits=[("sv", cnt["sv"])] if c == 0 else [], inc="sp")
            op("v", (lambda c=c: lambda e: e.tensor_copy(
                pnbc[:, c:c + 1], ps[:, 4 * 512 + 64 * c:4 * 512 + 64 * c + 1]))(),
                waits=[("sp", bt)])
        vspacer(inc="sv")
        tap("dpi", hm16[0:16, :], rearr=False)
        tap("dptm", Pi_tm[:, :], rearr=False)
        if PHASE_LIMIT < 8:
            return nc, ctx

        k = 0
        stt_t = {}
        for c in range(C):
            for j in range(J):
                w = [("sv", stt_t.get(k - 2, 0))] if k >= 2 else (
                    [("sv", hm_t[31])] if k == 0 else [])
                bt = op("t", (lambda c=c, j=j, k=k: lambda e: e.matmul(
                    bank(k % 2), selb[:, 128 * c:128 * c + 128],
                    hm16[0:16, 512 * j:512 * j + 512], start=True, stop=True))(),
                    waits=w, inc="sp")
                op("v", (lambda c=c, j=j: lambda e: e.tensor_tensor(
                    wsq0, xg(c, j), xg(c, j), ALU.mult))())
                op("v", (lambda c=c, j=j, k=k: lambda e: e.scalar_tensor_tensor(
                    junkf, wsq0, 1.0, bank(k % 2), ALU.mult, ALU.mult))(),
                    waits=[("sp", bt)])
                stt_t[k] = op("v", (lambda c=c, j=j: lambda e: e.tensor_reduce(
                    dotpart[:, 8 * c + j:8 * c + j + 1], junkf, AX.X, ALU.add))(),
                    inc="sv")
                k += 1
        op("v", lambda e: e.tensor_reduce(
            d8[:, :], dotpart[:, :].rearrange("p (c j) -> p c j", j=8),
            AX.X, ALU.add))
        vspacer()
        op("v", lambda e: e.tensor_tensor(d8[:, :], d8[:, :], pnbc[:, :], ALU.mult))
        vspacer()
        op("v", lambda e: e.tensor_scalar(d8[:, :], d8[:, :], 1.0, 1.0,
                                          ALU.mult, ALU.add))
        vspacer()
        op("v", lambda e: e.reciprocal(d8[:, :], d8[:, :]))
        vspacer()
        op("v", lambda e: e.tensor_scalar(negattn[:, :], d8[:, :], -1.0, 0.0,
                                          ALU.mult, ALU.add))
        vspacer(inc="sv")
        tap("dna", negattn[:, :], rearr=False)
        if PHASE_LIMIT < 9:
            return nc, ctx

        op("g", g_relay("dg", tok_cbt))
        cnt["dr"] += 1
        tok_R3 = cnt["dr"]
        oe_t = {}
        sg_t = {}
        acc_t = {}
        for j in range(J):
            for c in range(C):
                w = [("sv", stt_t[63])] if (j == 0 and c == 0) else []
                if c >= 2:
                    w.append(("sv", oe_t[(j, c - 2)]))
                elif j > 0:
                    w.append(("sv", oe_t[(j - 1, C - 2 + c)]))
                if j > 0:
                    w.append(("sp", sproj_last[0]))
                bt = op("t", (lambda c=c, j=j: lambda e: e.matmul(
                    bank(c % 2), selb[:, 128 * c:128 * c + 128],
                    hm16[0:16, 512 * j:512 * j + 512], start=True, stop=True))(),
                    waits=w, inc="sp")
                oe_t[(j, c)] = op("v", (lambda c=c, j=j: lambda e: e.scalar_tensor_tensor(
                    pbuf[:, 512 * c:512 * c + 512], xg(c, j), negattn[:, c:c + 1],
                    bank(c % 2), ALU.mult, ALU.mult))(),
                    waits=[("sp", bt)], inc="sv")
            lastmm = None
            for o in range(C):
                w = [("sv", oe_t[(j, C - 1)])] if o == 0 else []
                if o == 0 and j == 0:
                    w.append(("dr", tok_R3))
                if o >= 2:
                    w.append(("sv", sg_t[(j, o - 2)]))
                elif j > 0:
                    w.append(("sv", sg_t[(j - 1, C - 2 + o)]))
                for a in range(C):
                    lastmm = op("t", (lambda a=a, o=o: lambda e: e.matmul(
                        bank(2 + o % 2), warA[:, a * 1024 + o * 128:a * 1024 + o * 128 + 128],
                        pbuf[:, 512 * a:512 * a + 512],
                        start=(a == 0), stop=(a == C - 1)))(),
                        waits=(w if a == 0 else []),
                        inc=("sp" if a == C - 1 else None))
                ww = [("sp", lastmm)]
                if j >= 1:
                    ww.append(("dg", acc_t[j - 1]))
                sg_t[(j, o)] = op("v", (lambda o=o, j=j: lambda e: e.scalar_tensor_tensor(
                    sg_f[:, 512 * o:512 * o + 512], bank(2 + o % 2), bout_s[:, o:o + 1],
                    wg(o, j), ALU.add, ALU.mult))(), waits=ww, inc="sv")
            sproj_last = [lastmm]
            w = [("sv", sg_t[(j, C - 1)])]
            op("g", (lambda j=j: lambda e: e.dma_start(
                sg_f[:, :].rearrange("p (c t) -> p c t", c=C),
                xin_r[:, :, 512 * j:512 * j + 512], accum_op=ALU.add))(),
                waits=w, inc="dg", k=16)
            acc_t[j] = op("g", (lambda j=j: lambda e: e.dma_start(
                x1s_r[:, :, 512 * j:512 * j + 512],
                sg_f[:, :].rearrange("p (c t) -> p c t", c=C)))(),
                inc="dg", k=16)
        tok_x1done = acc_t[J - 1]
        P.block()
        if PHASE_LIMIT < 10:
            return nc, ctx

        tok_xout_init = [0]
        tok_tables = [0]
        tok_w1out = [0]

        def b7_extra():
            op("y", lambda e: e.dma_start(sgx[:, 0:4096], cosvd[:, :]), inc="ds", k=16)
            tok_tables[0] = op("y", lambda e: e.dma_start(sgx[:, 4096:8192],
                                                          sinvd[:, :]), inc="ds", k=16)
            tok_w1out[0] = op("y", lambda e: e.dma_start(
                warA[:, :].rearrange("p (a o) -> p a o", a=C),
                w1out.rearrange("(a p) o -> p a o", p=128)), inc="ds", k=16)
        b6_mm_done = sproj_last[0]
        ln_phase(x1s_r, extra=b7_extra,
                 jwaits=lambda j: [("dg", acc_t[j]), ("sp", b6_mm_done)])

        tap("dxn1", xng[:, :].rearrange("p (c t) -> p c t", c=C))
        if PHASE_LIMIT < 11:
            return nc, ctx

        QSB, KSB, QP, KP = 0, 4096, 8192, 12288
        VTO, AOF, RSC = 16384, 22528, 26624
        qsb = wbuf[:, QSB:QSB + 4096]
        ksb = wbuf[:, KSB:KSB + 4096]
        Qp_t = wbuf[:, QP:QP + 4096]
        Kp_t = wbuf[:, KP:KP + 4096]
        aof = wbuf[:, AOF:AOF + 4096]
        rsc1 = wbuf[:, RSC:RSC + 1024].bitcast(F32)
        rsc2 = wbuf[:, RSC + 1024:RSC + 2048].bitcast(F32)
        cosv_sb = sgx[:, 0:4096]
        sinv_sb = sgx[:, 4096:8192]
        wsl = xbf
        pA = pbuf[:, 0:2048]
        pB = pbuf[:, 2048:4096]

        def vto_ap(tt):
            return bass.AP(wbuf, VTO + 192 * tt, [[32768, 128], [128, 2], [1, 64]])

        def vto_sl(tt, h):
            lo = VTO + 192 * tt + 64 * h
            return wbuf[:, lo:lo + 128]

        aof_dma_tok = [0]
        tok_vto_ones = [0]
        tok_R7 = [0]
        prev_vv31 = [0]
        for p in range(NPAIR):
            if p == 0:
                op("g", g_relay("dg", cnt["dg"]))
                cnt["dr"] += 1
                tok_R7[0] = cnt["dr"]
                tok_vto_ones[0] = op("g", lambda e: e.memset(
                    wbuf[:, VTO:VTO + 6144], 1.0), inc="pl")
            dws = None
            for qi, off in ((0, 128 * p), (1, D + 128 * p), (2, 2 * D + 128 * p)):
                wd = [("sv", prev_vv31[0])] if (qi == 0 and p > 0) else []
                dws = op("g", (lambda qi=qi, off=off: lambda e: e.dma_start(
                    wsl[:, 1024 * qi:1024 * qi + 1024].rearrange("p (a o) -> p a o", a=C),
                    w1qkv.rearrange("(a p) d -> p a d", p=128)[:, :, off:off + 128]))(),
                    waits=wd, inc="dg", k=16)
            ev_t = {}
            for qi in range(2):
                dst = qsb if qi == 0 else ksb
                for j in range(J):
                    kk = qi * J + j
                    w = [("dg", dws)] if kk == 0 else []
                    if kk >= 2:
                        w.append(("sv", ev_t[kk - 2]))
                    lm = None
                    for a in range(C):
                        lm = op("t", (lambda qi=qi, a=a, j=j, kk=kk: lambda e: e.matmul(
                            bank(kk % 2), wsl[:, 1024 * qi + a * 128:1024 * qi + a * 128 + 128],
                            xg(a, j), start=(a == 0), stop=(a == C - 1)))(),
                            waits=(w if a == 0 else []),
                            inc=("sp" if a == C - 1 else None))
                    ev_t[kk] = op("v", (lambda dst=dst, j=j, kk=kk: lambda e: e.tensor_copy(
                        dst[:, 512 * j:512 * j + 512], bank(kk % 2)))(),
                        waits=[("sp", lm)], inc="sv")
            vv_t = {}
            for tt in range(32):
                w = []
                if tt == 0:
                    w = [("pl", tok_vto_ones[0]), ("dg", dws)]
                if tt >= 2:
                    w.append(("sv", vv_t[tt - 2]))
                lm = None
                for a in range(C):
                    lm = op("t", (lambda a=a, tt=tt: lambda e: e.matmul(
                        bank(2 + tt % 2, 128),
                        xng[:, 4096 * a + 128 * tt:4096 * a + 128 * tt + 128],
                        wsl[:, 2048 + a * 128:2048 + a * 128 + 128],
                        start=(a == 0), stop=(a == C - 1)))(),
                        waits=(w if a == 0 else []),
                        inc=("sp" if a == C - 1 else None))
                vv_t[tt] = op("v", (lambda tt=tt: lambda e: e.tensor_copy(
                    vto_ap(tt), bank(2 + tt % 2, 128)))(),
                    waits=[("sp", lm)], inc="sv")
            prev_vv31[0] = vv_t[31]
            rot_t = {}
            for qi in range(2):
                src = qsb if qi == 0 else ksb
                dst = Qp_t if qi == 0 else Kp_t
                for j in range(J):
                    kk = qi * J + j
                    w = [("sv", ev_t[kk])]
                    if p == 0 and kk == 0:
                        w.append(("ds", tok_tables[0]))
                    if kk >= 2:
                        w.append(("sv", rot_t[kk - 2]))
                    pm = op("t", (lambda src=src, j=j, kk=kk: lambda e: e.matmul(
                        bank(4 + kk % 2), permb[:, :], src[:, 512 * j:512 * j + 512],
                        start=True, stop=True))(), waits=w, inc="sp")
                    op("v", (lambda src=src, j=j: lambda e: e.tensor_tensor(
                        rsc1, src[:, 512 * j:512 * j + 512],
                        cosv_sb[:, 512 * j:512 * j + 512], ALU.mult))(),
                        waits=[("sv", ev_t[kk])])
                    op("v", (lambda j=j, kk=kk: lambda e: e.tensor_tensor(
                        rsc2, bank(4 + kk % 2), sinv_sb[:, 512 * j:512 * j + 512],
                        ALU.mult))(), waits=[("sp", pm)])
                    rot_t[kk] = op("v", (lambda dst=dst, j=j: lambda e: e.tensor_tensor(
                        dst[:, 512 * j:512 * j + 512], rsc1, rsc2, ALU.add))(),
                        inc="sv")
            if DEBUG and p == 0:
                op("y", lambda e: e.dma_start(dbg["dqp"][:, :], Qp_t[:, :]), inc="ds", k=16)
                op("y", lambda e: e.dma_start(dbg["dkp"][:, :], Kp_t[:, :]), inc="ds", k=16)
                op("y", lambda e: e.dma_start(dbg["dvt"][:, :], wbuf[:, VTO:VTO + 6144]),
                   inc="ds", k=16)
                op("y", lambda e: e.wait_ge(P.sems["ds"], cnt["ds"]))
                P.block()

            units = [(qt, kc) for qt in range(8) for kc in range(4 * qt + 4)]
            NU = len(units)
            exp_u = {}
            pv_u = {}
            evac_q = {}

            def PQ(ui):
                return pbuf[:, 1024 * (ui % 4):1024 * (ui % 4) + 1024]

            def emit_qk(ui):
                qt, kc = units[ui]
                c0 = max(0, kc * 128 - qt * 512)
                diag = 0 <= kc * 128 - qt * 512 < 512
                sbb = 2 * (ui % 2)
                w = []
                if ui >= 2:
                    w.append(("sa", exp_u[ui - 2]))
                if kc == 0:
                    w.append(("sv", rot_t[qt]))
                if qt == kc // 4:
                    w.append(("sv", rot_t[8 + kc // 4]))
                qk_tok = None
                for h in range(2):
                    ww = w if h == 0 else []
                    stop = not diag
                    qk_tok = op("t", (lambda h=h, c0=c0, kc=kc, qt=qt, sbb=sbb, stop=stop:
                                      lambda e: e.matmul(
                        ps[:, 512 * sbb + 512 * h + c0:512 * sbb + 512 * h + 512],
                        Kp_t[64 * h:64 * h + 64, 128 * kc:128 * kc + 128],
                        Qp_t[64 * h:64 * h + 64, 512 * qt + c0:512 * qt + 512],
                        start=True, stop=stop, tile_position=(64 * h, 0)))(),
                        waits=ww,
                        inc=("sp" if (h == 1 and not diag) else None))
                if diag:
                    for h in range(2):
                        qk_tok = op("t", (lambda h=h, c0=c0, sbb=sbb: lambda e: e.matmul(
                            ps[:, 512 * sbb + 512 * h + c0:512 * sbb + 512 * h + c0 + 128],
                            identb[:, :], maskb[:, :], start=False, stop=True))(),
                            inc=("sp" if h == 1 else None))
                return qk_tok

            qk_toks = {0: emit_qk(0), 1: emit_qk(1)}
            for ui, (qt, kc) in enumerate(units):
                c0 = max(0, kc * 128 - qt * 512)
                sbb = 2 * (ui % 2)
                pbb = PQ(ui)
                aob = 4 + 2 * (qt % 2)
                first = kc == 0
                last = kc == 4 * qt + 3
                w = [("sp", qk_toks[ui])]
                if ui >= 4:
                    w.append(("sp", pv_u[ui - 4]))
                if c0 == 0:
                    exp_u[ui] = op("s", (lambda sbb=sbb, pbb=pbb: lambda e: e.activation(
                        pbb[:, 0:1024], ps[:, 512 * sbb:512 * sbb + 1024], AF.Exp,
                        bias=f32(-8.0), scale=f32(0.125)))(), waits=w, inc="sa")
                else:
                    op("s", (lambda sbb=sbb, pbb=pbb, c0=c0: lambda e: e.activation(
                        pbb[:, c0:512], ps[:, 512 * sbb + c0:512 * sbb + 512], AF.Exp,
                        bias=f32(-8.0), scale=f32(0.125)))(), waits=w)
                    exp_u[ui] = op("s", (lambda sbb=sbb, pbb=pbb, c0=c0: lambda e: e.activation(
                        pbb[:, 512 + c0:1024], ps[:, 512 * sbb + 512 + c0:512 * sbb + 1024],
                        AF.Exp, bias=f32(-8.0), scale=f32(0.125)))(), inc="sa")
                for h in range(2):
                    w = [("sa", exp_u[ui])] if h == 0 else []
                    if h == 0 and qt == kc // 4:
                        w.append(("sv", vv_t[kc]))
                    if h == 0 and first and qt >= 2:
                        w.append(("sv", evac_q[qt - 2]))
                    pv_u[ui] = op("t", (lambda h=h, kc=kc, qt=qt, c0=c0, pbb=pbb, aob=aob,
                                        first=first, last=last: lambda e: e.matmul(
                        ps[:, 512 * (aob + h) + c0:512 * (aob + h) + 512],
                        vto_sl(kc, h), pbb[:, 512 * h + c0:512 * h + 512],
                        start=first, stop=last))(),
                        waits=w, inc=("sp" if h == 1 else None))
                if ui + 2 < NU:
                    qk_toks[ui + 2] = emit_qk(ui + 2)
                if last:
                    et = None
                    for h in range(2):
                        aorow = 0 if h == 0 else 64
                        denrow = 64 if h == 0 else 0
                        w = [("sp", pv_u[ui])]
                        if p > 0 and qt == 0 and h == 0:
                            w.append(("ds", aof_dma_tok[0]))
                        op("v", (lambda h=h, aob=aob, denrow=denrow: lambda e: e.reciprocal(
                            rbcsb[:, :],
                            ps[denrow:denrow + 64, 512 * (aob + h):512 * (aob + h) + 512]))(),
                            waits=w)
                        et = op("v", (lambda h=h, qt=qt, aob=aob, aorow=aorow:
                                      lambda e: e.tensor_tensor(
                            aof[64 * h:64 * h + 64, 512 * qt:512 * qt + 512],
                            ps[aorow:aorow + 64, 512 * (aob + h):512 * (aob + h) + 512],
                            rbcsb[:, :], ALU.mult))(), inc="sv")
                    evac_q[qt] = et
            aof_dma_tok[0] = op("y", (lambda p=p: lambda e: e.dma_start(
                aos[128 * p:128 * p + 128, :], aof[:, :]))(),
                waits=[("sv", evac_q[7])], inc="ds", k=16)
        P.block()
        if PHASE_LIMIT < 12:
            return nc, ctx

        fsc = xbf[:, 0:1024].bitcast(F32)
        g1_t = {}
        ffn_vt = {}
        sg1_t = {}
        acc1_t = {}
        ldj = {}
        sproj_done = {}
        for j in range(J):
            w = [] if j == 0 else [("sp", sproj_done[j - 1])]
            ld0 = op("y", (lambda j=j: lambda e: e.dma_start(
                pbuf[:, :].rearrange("p (c t) -> p c t", c=C),
                aos_r[:, :, 512 * j:512 * j + 512]))(), waits=w, inc="ds", k=16)
            ldj[j] = ld0
            for o in range(C):
                w = []
                if j == 0 and o == 0:
                    w += [("dr", tok_R3), ("ds", ldj[0] if 0 in ldj else 1)]
                if o >= 2:
                    w.append(("sv", ffn_vt[(j, o - 2)]))
                elif j > 0:
                    w.append(("sv", ffn_vt[(j - 1, C - 2 + o)]))
                lm = None
                for a in range(C):
                    lm = op("t", (lambda a=a, o=o, j=j: lambda e: e.matmul(
                        bank(o % 2), warB[:, a * 1024 + o * 128:a * 1024 + o * 128 + 128],
                        xg(a, j), start=(a == 0), stop=(a == C - 1)))(),
                        waits=(w if a == 0 else []),
                        inc=("sp" if a == C - 1 else None))
                wv = [("sp", lm)]
                prev = (j, o - 1) if o > 0 else ((j - 1, C - 1) if j > 0 else None)
                if prev is not None:
                    wv.append(("sa", g1_t[prev]))
                ffn_vt[(j, o)] = op("v", (lambda o=o: lambda e: e.tensor_scalar(
                    fsc, bank(o % 2), scb_s[:, o:o + 1], bias1_s[:, o:o + 1],
                    ALU.mult, ALU.add))(), waits=wv, inc="sv")
                g1_t[(j, o)] = op("s", (lambda o=o, j=j: lambda e: e.activation(
                    wg(o, j), fsc, AF.Silu))(),
                    waits=[("sv", ffn_vt[(j, o)])], inc="sa")
            lastmm = None
            for o in range(C):
                w = [("ds", ldj[j])] if o == 0 else []
                if o == 0 and j == 0:
                    w.append(("ds", tok_w1out[0]))
                if o >= 2:
                    w.append(("sv", sg1_t[(j, o - 2)]))
                elif j > 0:
                    w.append(("sv", sg1_t[(j - 1, C - 2 + o)]))
                for a in range(C):
                    lastmm = op("t", (lambda a=a, o=o: lambda e: e.matmul(
                        bank(2 + o % 2), warA[:, a * 1024 + o * 128:a * 1024 + o * 128 + 128],
                        pbuf[:, 512 * a:512 * a + 512],
                        start=(a == 0), stop=(a == C - 1)))(),
                        waits=(w if a == 0 else []),
                        inc=("sp" if a == C - 1 else None))
                ww = [("sp", lastmm), ("sa", g1_t[(j, o)])]
                if j >= 1:
                    ww.append(("dg", acc1_t[j - 1]))
                sg1_t[(j, o)] = op("v", (lambda o=o, j=j: lambda e: e.scalar_tensor_tensor(
                    sg_f[:, 512 * o:512 * o + 512], bank(2 + o % 2), 1.0,
                    wg(o, j), ALU.mult, ALU.mult))(), waits=ww, inc="sv")
            sproj_done[j] = lastmm
            w = [("sv", sg1_t[(j, C - 1)])]
            op("g", (lambda j=j: lambda e: e.dma_start(
                sg_f[:, :].rearrange("p (c t) -> p c t", c=C),
                x1s_r[:, :, 512 * j:512 * j + 512], accum_op=ALU.add))(),
                waits=w, inc="dg", k=16)
            acc1_t[j] = op("g", (lambda j=j: lambda e: e.dma_start(
                xout_r[:, :, 512 * j:512 * j + 512],
                sg_f[:, :].rearrange("p (c t) -> p c t", c=C)))(),
                inc="dg", k=16)
        op("g", lambda e: e.wait_ge(P.sems["dg"], cnt["dg"]))
        P.block()

    return nc, None


def _host_prep(inputs):
    L0x = np.asarray(inputs["L0_attn_x"], dtype=np.float32)
    B = L0x.shape[0]
    pos = np.arange(T, dtype=np.float32)[:, None]
    ii = np.arange(0, 64, 2, dtype=np.float32)[None, :]
    theta = (pos / np.power(f32(10000.0), ii / f32(64.0))).astype(np.float32)
    cost = np.cos(theta).astype(np.float32)
    sint = np.sin(theta).astype(np.float32)
    dd = (np.arange(128) % 64) % 32
    cosv = np.ascontiguousarray(cost[:, dd].T)
    sinv = np.ascontiguousarray(sint[:, dd].T)
    perm = np.zeros((128, 128), dtype=np.float32)
    for jj in range(128):
        base = (jj // 64) * 64
        d = jj % 64
        if d < 32:
            perm[base + d + 32, jj] = -1.0
        else:
            perm[base + d - 32, jj] = 1.0
    ident = np.eye(128, dtype=np.float32)
    mask = np.where(np.arange(128)[:, None] > np.arange(128)[None, :],
                    np.float32(-1e9), np.float32(0.0)).astype(np.float32)
    sel = np.zeros((16, D), dtype=np.float32)
    for ff in range(D):
        sel[2 * (ff // 128) + (ff % 128) // 64, ff] = 1.0

    common = {
        "w0qkvT": np.ascontiguousarray(np.asarray(inputs["L0_attn_Wqkv"], np.float32).T.astype(ml_dtypes.bfloat16)),
        "w0outT": np.ascontiguousarray(np.asarray(inputs["L0_attn_Wout"], np.float32).T),
        "tempd": np.ascontiguousarray(np.asarray(inputs["L0_attn_temp"], np.float32)),
        "boutd": np.ascontiguousarray(
            np.asarray(inputs["L0_attn_bout"], np.float32).reshape(8, 128).T),
        "th0": np.ascontiguousarray(np.asarray(inputs["L0_ffn_theta"], np.float32)),
        "thn0": np.ascontiguousarray(np.asarray(inputs["L0_ffn_theta_noise"], np.float32)),
        "ph0": np.ascontiguousarray(np.asarray(inputs["L0_ffn_phi"], np.float32)),
        "phn0": np.ascontiguousarray(np.asarray(inputs["L0_ffn_phi_noise"], np.float32)),
        "w1qkv": np.ascontiguousarray(np.asarray(inputs["L1_attn_w_qkv"], np.float32)),
        "w1out": np.ascontiguousarray(np.asarray(inputs["L1_attn_w_out"], np.float32).astype(ml_dtypes.bfloat16)),
        "cbT": np.ascontiguousarray(np.asarray(inputs["L1_ffn_CB"], np.float32).T),
        "scbd": np.ascontiguousarray(
            np.asarray(inputs["L1_ffn_SCB"], np.float32).reshape(8, 128).T),
        "bias1d": np.ascontiguousarray(
            np.asarray(inputs["L1_ffn_bias"], np.float32).reshape(8, 128).T),
        "cosvd": cosv.astype(ml_dtypes.bfloat16), "sinvd": sinv.astype(ml_dtypes.bfloat16), "permd": perm, "identd": ident,
        "maskd": mask, "seld": sel,
    }
    in_maps = []
    for b in range(B):
        m = dict(common)
        m["xin"] = np.ascontiguousarray(L0x[b].T)
        in_maps.append(m)
    return in_maps


def kernel(**inputs):
    in_maps = _host_prep(inputs)
    nc, _ = build()
    res = run_bass_kernel_spmd(nc, in_maps, core_ids=list(range(8)))
    out = np.stack([np.ascontiguousarray(r["xout"].T) for r in res.results], axis=0)
    return out.astype(np.float32)


# version 16
# speedup vs baseline: 1.1263x; 1.0030x over previous
import numpy as np
import ml_dtypes
from contextlib import ExitStack
import concourse.bass as bass
import concourse.mybir as mybir
from concourse.bass_utils import run_bass_kernel_spmd

F32 = mybir.dt.float32
BF16 = mybir.dt.bfloat16
AF = mybir.ActivationFunctionType
ALU = mybir.AluOpType
AX = mybir.AxisListType
f32 = np.float32

DEBUG = False
PHASE_LIMIT = 99

T = 4096
D = 1024
J = 8
C = 8
NPAIR = 8


class Prog:
    ENGS = ("t", "s", "v", "g", "y")

    def __init__(self, nc, ctx):
        self.nc = nc
        self.q = {k: [] for k in self.ENGS}
        self.sems = {}
        self.cnt = {}
        for n in ("sp", "sa", "sv", "dg", "ds", "pl", "dr", "dy"):
            self.sems[n] = ctx.enter_context(nc.semaphore(n))
            self.cnt[n] = 0

    def op(self, eng, fn, waits=(), inc=None, k=1):
        tok = None
        if inc is not None:
            self.cnt[inc] += k
            tok = self.cnt[inc]
        sems = self.sems
        wl = [(sems[s], v) for (s, v) in waits if v and v > 0]
        incs = (sems[inc], k) if inc is not None else None

        def run(e):
            for (h, v) in wl:
                e.wait_ge(h, v)
            ins = fn(e)
            if incs is not None:
                ins.then_inc(incs[0], incs[1])
        self.q[eng].append(run)
        return tok

    def block(self):
        nc = self.nc
        q = self.q
        with nc.Block() as blk:
            if q["g"]:
                @blk.gpsimd
                def _(e):
                    for fn in q["g"]:
                        fn(e)
            if q["t"]:
                @blk.tensor
                def _(e):
                    for fn in q["t"]:
                        fn(e)
            if q["s"]:
                @blk.scalar
                def _(e):
                    for fn in q["s"]:
                        fn(e)
            if q["v"]:
                @blk.vector
                def _(e):
                    for fn in q["v"]:
                        fn(e)
            if q["y"]:
                @blk.sync
                def _(e):
                    for fn in q["y"]:
                        fn(e)
        self.q = {k: [] for k in self.ENGS}


def build():
    nc = bass.Bass("TRN2", target_bir_lowering=False)

    xin = nc.dram_tensor("xin", [D, T], F32, kind="ExternalInput")
    w0qkvT = nc.dram_tensor("w0qkvT", [D, D], BF16, kind="ExternalInput")
    w0outT = nc.dram_tensor("w0outT", [D, D], F32, kind="ExternalInput")
    tempd = nc.dram_tensor("tempd", [16, 1], F32, kind="ExternalInput")
    boutd = nc.dram_tensor("boutd", [128, 8], F32, kind="ExternalInput")
    th0 = nc.dram_tensor("th0", [D, D], F32, kind="ExternalInput")
    thn0 = nc.dram_tensor("thn0", [D, D], F32, kind="ExternalInput")
    ph0 = nc.dram_tensor("ph0", [D, D], F32, kind="ExternalInput")
    phn0 = nc.dram_tensor("phn0", [D, D], F32, kind="ExternalInput")
    w1qkv = nc.dram_tensor("w1qkv", [D, 3 * D], F32, kind="ExternalInput")
    w1out = nc.dram_tensor("w1out", [D, D], BF16, kind="ExternalInput")
    cbT = nc.dram_tensor("cbT", [D, D], F32, kind="ExternalInput")
    scbd = nc.dram_tensor("scbd", [128, 8], F32, kind="ExternalInput")
    bias1d = nc.dram_tensor("bias1d", [128, 8], F32, kind="ExternalInput")
    cosvd = nc.dram_tensor("cosvd", [128, T], BF16, kind="ExternalInput")
    sinvd = nc.dram_tensor("sinvd", [128, T], BF16, kind="ExternalInput")
    permd = nc.dram_tensor("permd", [128, 128], F32, kind="ExternalInput")
    identd = nc.dram_tensor("identd", [128, 128], F32, kind="ExternalInput")
    maskd = nc.dram_tensor("maskd", [128, 128], F32, kind="ExternalInput")
    seld = nc.dram_tensor("seld", [16, D], F32, kind="ExternalInput")

    ikind = "ExternalOutput"
    x1s = nc.dram_tensor("x1s", [D, T], F32, kind=ikind)
    aos = nc.dram_tensor("aos", [D, T], BF16, kind=ikind)
    xout = nc.dram_tensor("xout", [D, T], F32, kind="ExternalOutput")
    dbg = {}
    if DEBUG:
        for nm, shape, dt in (("dxn", [D, T], BF16), ("dw", [D, T], BF16),
                              ("dgg", [D, T], BF16), ("dpi", [16, T], BF16),
                              ("dna", [128, 8], F32), ("dxn1", [D, T], BF16),
                              ("dqp", [128, T], BF16), ("dkp", [128, T], BF16),
                              ("dvt", [128, 6144], BF16), ("dcw", [128, 8192], BF16),
                              ("dsw", [128, 8192], BF16), ("dlg", [16, T], BF16),
                              ("dptm", [128, 512], BF16), ("dnsq", [128, 64], F32),
                              ("dinv", [128, 128], BF16)):
            dbg[nm] = nc.dram_tensor(nm, shape, dt, kind="ExternalOutput")

    rr = lambda d: d.rearrange("(c p) t -> p c t", p=128)
    xin_r, x1s_r, xout_r, aos_r = rr(xin), rr(x1s), rr(xout), rr(aos)

    with ExitStack() as ctx:
        sbt = lambda name, shape, dt: ctx.enter_context(nc.sbuf_tensor(name, shape, dt))

        xng = sbt("xng", [128, 32768], BF16)
        wbuf = sbt("wbuf", [128, 32768], BF16)
        warA = sbt("warA", [128, 8192], BF16)
        warB = sbt("warB", [128, 8192], BF16)
        sgx = sbt("sgx", [128, 8192], BF16)
        xbf = sbt("xbf", [128, 4096], BF16)
        pbuf = sbt("pbuf", [128, 4096], BF16)
        identb = sbt("identb", [128, 128], BF16)
        permb = sbt("permb", [128, 128], BF16)
        maskb = sbt("maskb", [128, 128], BF16)
        selb = sbt("selb", [16, 1024], BF16)
        temp_s = sbt("temp_s", [16, 1], F32)
        bout_s = sbt("bout_s", [128, 8], F32)
        scb_s = sbt("scb_s", [128, 8], F32)
        bias1_s = sbt("bias1_s", [128, 8], F32)
        ones_bf = sbt("ones_bf", [128, 1], BF16)
        onesr_bf = sbt("onesr_bf", [1, 128], BF16)
        rows_f = sbt("rows_f", [1, 1024], F32)
        rows_b = sbt("rows_b", [1, 1024], BF16)
        nsqpart = sbt("nsqpart", [128, 64], F32)
        inv2b = sbt("inv2b", [128, 8], BF16)
        selinv = sbt("selinv", [128, 128], BF16)
        dotpart = sbt("dotpart", [128, 64], F32)
        negattn = sbt("negattn", [128, 8], F32)
        pnbc = sbt("pnbc", [128, 8], F32)
        pn_f = sbt("pn_f", [16, 1], F32)
        pn_b = sbt("pn_b", [16, 1], BF16)
        Pi_tm = sbt("Pi_tm", [128, 512], BF16)
        negmax = sbt("negmax", [128, 1], F32)
        ssum = sbt("ssum", [128, 1], F32)
        sinvv = sbt("sinvv", [128, 1], F32)
        t128a = sbt("t128a", [128, 1], F32)
        t128b = sbt("t128b", [128, 1], F32)
        rbcsb = sbt("rbcsb", [64, 512], F32)
        m32 = sbt("m32", [128, 32], F32)
        d8 = sbt("d8", [128, 8], F32)
        ps = ctx.enter_context(nc.psum_tensor("ps", [128, 4096], F32))

        hm16 = xbf
        sg_f = sgx[:, 0:8192].bitcast(F32)
        sc1 = pbuf[:, 0:1024].bitcast(F32)
        sc2 = pbuf[:, 1024:2048].bitcast(F32)
        wsq0 = pbuf[:, 2048:2560]
        wsq1 = pbuf[:, 2560:3072]
        junkf = pbuf[:, 3072:4096].bitcast(F32)
        ln_sc = wbuf[:, 0:1024].bitcast(F32)
        lnq0 = wbuf[:, 1024:1536]
        lnq1 = wbuf[:, 1536:2048]

        def bank(b, w=512):
            return ps[:, 512 * b:512 * b + w]

        P = Prog(nc, ctx)
        op = P.op
        cnt = P.cnt

        def g_relay(sem_from, tok):
            def f(e):
                e.wait_ge(P.sems[sem_from], tok)
                return e.sem_inc(P.sems["dr" if sem_from == "dg" else "dy"], 1)
            return f

        def vspacer(inc=None):
            return op("v", lambda e: e.tensor_scalar(
                pbuf[:, 1024:2048].bitcast(F32), pbuf[:, 1024:2048].bitcast(F32),
                1.0, 0.0, ALU.mult, ALU.add), inc=inc)
        PI2 = f32(np.pi / 2.0)

        def xg(c, j, w=512, base=128 * 0):
            return xng[:, 4096 * c + 512 * j:4096 * c + 512 * j + w]

        def wg(c, j, w=512):
            return wbuf[:, 4096 * c + 512 * j:4096 * c + 512 * j + w]

        op("y", lambda e: e.dma_start(temp_s[:, :], tempd[:, :]), inc="ds", k=16)
        op("y", lambda e: e.dma_start(bout_s[:, :], boutd[:, :]), inc="ds", k=16)
        op("y", lambda e: e.dma_start(scb_s[:, :], scbd[:, :]), inc="ds", k=16)
        op("y", lambda e: e.dma_start(bias1_s[:, :], bias1d[:, :]), inc="ds", k=16)
        tok_sconsts = cnt["ds"]
        op("g", lambda e: e.dma_start(identb[:, :], identd[:, :]), inc="dg", k=16)
        op("g", lambda e: e.dma_start(permb[:, :], permd[:, :]), inc="dg", k=16)
        op("g", lambda e: e.dma_start(maskb[:, :], maskd[:, :]), inc="dg", k=16)
        op("g", lambda e: e.dma_start(selb[:, :], seld[:, :]), inc="dg", k=16)
        tok_gconsts = cnt["dg"]
        op("g", lambda e: e.memset(ones_bf[:, :], 1.0))
        op("g", lambda e: e.memset(onesr_bf[:, :], 1.0))
        tok_selinv0 = op("g", lambda e: e.memset(selinv[:, :], 0.0), inc="pl")
        op("v", lambda e: e.tensor_scalar(scb_s[:, :], scb_s[:, :], 1.0 / 127.0, 0.0,
                                          ALU.mult, ALU.add),
           waits=[("ds", tok_sconsts)], inc="sv")

        sin_toks = []
        for si, (src, nsrc, dst, bias) in enumerate(
                ((th0, thn0, warA, PI2), (ph0, phn0, warB, f32(0.0)))):
            for a in range(C):
                sl = (si * C + a) % 2
                A = sg_f[:, 2048 * sl:2048 * sl + 1024]
                B = sg_f[:, 2048 * sl + 1024:2048 * sl + 2048]
                w = []
                if len(sin_toks) >= 2:
                    w = [sin_toks[-2]]
                dt1 = op("y", (lambda A=A, src=src, a=a: lambda e: e.dma_start(
                    A, src.rearrange("(a p) o -> p a o", p=128)[:, a, :]))(),
                    waits=w, inc="ds", k=16)
                dt2 = op("s", (lambda B=B, nsrc=nsrc, a=a: lambda e: e.dma_start(
                    B, nsrc.rearrange("(a p) o -> p a o", p=128)[:, a, :]))(),
                    waits=w, inc="dy", k=16)
                vt = op("v", (lambda A=A, B=B: lambda e: e.tensor_tensor(A, A, B, ALU.add))(),
                        waits=[("ds", dt1), ("dy", dt2)], inc="sv")
                if si == 0:
                    st = op("s", (lambda A=A, B=B: lambda e: e.activation(
                        B, A, AF.Sin, bias=PI2, scale=f32(1.0)))(),
                        waits=[("sv", vt)], inc="sa")
                    ts2 = op("v", (lambda dst=dst, a=a, B=B: lambda e: e.tensor_scalar(
                        dst[:, a * 1024:(a + 1) * 1024], B, 1.0, -1.0,
                        ALU.mult, ALU.add))(), waits=[("sa", st)], inc="sv")
                    sin_toks.append(("sv", ts2))
                else:
                    st = op("s", (lambda dst=dst, a=a, A=A, bias=bias: lambda e: e.activation(
                        dst[:, a * 1024:(a + 1) * 1024], A, AF.Sin, bias=bias,
                        scale=f32(1.0)))(), waits=[("sv", vt)], inc="sa")
                    sin_toks.append(("sa", st))
        tok_w0qkv = op("y", lambda e: e.dma_start(
            sgx[:, :].rearrange("p (a o) -> p a o", a=C),
            w0qkvT.rearrange("(a p) o -> p a o", p=128)),
            waits=[sin_toks[14], sin_toks[15]], inc="ds", k=16)


        def ln_phase(src_r, extra=None, jwaits=None):
            if extra:
                extra()
            xn_done = [0] * J
            for j in range(J):
                buf = xbf if j % 2 == 0 else pbuf
                w = [("sv", xn_done[j - 2])] if j >= 2 else []
                if jwaits:
                    w = w + jwaits(j)
                dt0 = op("g", (lambda buf=buf, j=j: lambda e: e.dma_start(
                    buf[:, :].rearrange("p (c t) -> p c t", c=C),
                    src_r[:, :, 512 * j:512 * j + 512]))(), waits=w, inc="dg", k=16)
                dt = ("dg", dt0)
                squse = [0] * C
                for c in range(C):
                    dst = lnq0 if c % 2 == 0 else lnq1
                    w = [dt] if c == 0 else []
                    if c >= 2:
                        w.append(("sp", squse[c - 2]))
                    sq = op("s", (lambda dst=dst, buf=buf, c=c: lambda e: e.activation(
                        dst, buf[:, 512 * c:512 * c + 512], AF.Square))(),
                        waits=w, inc="sa")
                    w2 = [dt] if c == 0 else []
                    op("t", (lambda buf=buf, c=c: lambda e: e.matmul(
                        ps[0:1, 4 * 512:4 * 512 + 512], ones_bf[:, :],
                        buf[:, 512 * c:512 * c + 512], start=(c == 0), stop=(c == C - 1)))(),
                        waits=w2)
                    squse[c] = op("t", (lambda dst=dst, c=c: lambda e: e.matmul(
                        ps[0:1, 5 * 512:5 * 512 + 512], ones_bf[:, :], dst,
                        start=(c == 0), stop=(c == C - 1)))(),
                        waits=[("sa", sq)], inc="sp")
                w = [("sp", squse[C - 1])]
                if j > 0:
                    w.append(("sp", bc_last[0]))
                op("v", lambda e: e.tensor_scalar(rows_f[0:1, 0:512],
                                                  ps[0:1, 4 * 512:4 * 512 + 512],
                                                  -1.0 / 1024.0, 0.0, ALU.mult, ALU.add),
                   waits=w)
                op("v", lambda e: e.scalar_tensor_tensor(
                    rows_f[0:1, 512:1024], rows_f[0:1, 0:512], 1.0,
                    rows_f[0:1, 0:512], ALU.mult, ALU.mult))
                vart = op("v", lambda e: e.scalar_tensor_tensor(
                    rows_f[0:1, 512:1024], ps[0:1, 5 * 512:5 * 512 + 512],
                    1.0 / 1024.0, rows_f[0:1, 512:1024], ALU.mult, ALU.subtract),
                    inc="sv")
                sdt = op("s", lambda e: e.activation(rows_f[0:1, 512:1024],
                                                     rows_f[0:1, 512:1024],
                                                     AF.Sqrt, bias=f32(1e-5)),
                         waits=[("sv", vart)], inc="sa")
                op("v", lambda e: e.reciprocal(rows_f[0:1, 512:1024],
                                               rows_f[0:1, 512:1024]),
                   waits=[("sa", sdt)])
                rbt = op("v", lambda e: e.tensor_copy(rows_b[0:1, 512:1024],
                                                      rows_f[0:1, 512:1024]))
                rbt = op("v", lambda e: e.tensor_copy(rows_b[0:1, 0:512],
                                                      rows_f[0:1, 0:512]), inc="sv")
                wrt = [("sv", rbt)]
                if j > 0:
                    wrt.append(("sv", evt_last[0]))
                rt = op("t", lambda e: e.matmul(bank(2), onesr_bf[:, :],
                                                rows_b[0:1, 512:1024],
                                                start=True, stop=True),
                        waits=wrt, inc="sp")
                evt = op("v", lambda e: e.tensor_copy(ln_sc, bank(2)),
                         waits=[("sp", rt)], inc="sv")
                evt_last[0] = evt
                xnt = [0] * C
                for c in range(C):
                    w = []
                    if c >= 2:
                        w.append(("sv", xnt[c - 2]))
                    elif j > 0:
                        w.append(("sv", xn_done[j - 1]))
                    op("t", (lambda c=c: lambda e: e.matmul(
                        bank(c % 2), onesr_bf[:, :], rows_b[0:1, 0:512],
                        start=True, stop=False))(), waits=w)
                    bt = op("t", (lambda buf=buf, c=c: lambda e: e.matmul(
                        bank(c % 2), identb[:, :], buf[:, 512 * c:512 * c + 512],
                        start=False, stop=True))(), inc="sp")
                    ww = [("sp", bt)]
                    if c == 0:
                        ww.append(("sv", evt))
                    xnt[c] = op("v", (lambda c=c, j=j: lambda e: e.tensor_tensor(
                        xg(c, j), bank(c % 2), ln_sc, ALU.mult))(),
                        waits=ww, inc="sv")
                bc_last[0] = bt
                xn_done[j] = xnt[C - 1]
            P.block()
            return ("sv", xn_done[J - 1])

        bc_last = [0]
        evt_last = [0]

        if DEBUG:
            op("y", lambda e: e.dma_start(dbg["dcw"][:, :], warA[:, :]), inc="ds", k=16)
            op("y", lambda e: e.dma_start(dbg["dsw"][:, :], warB[:, :]), inc="ds", k=16)
            op("y", lambda e: e.wait_ge(P.sems["ds"], cnt["ds"]))
            P.block()
        ln0_done = ln_phase(xin_r)

        def tap(name, src_ap, rearr=True):
            if not DEBUG:
                return
            d = dbg[name]
            if rearr:
                op("y", lambda e: e.dma_start(d.rearrange("(c p) t -> p c t", p=128),
                                              src_ap), inc="ds", k=16)
            else:
                op("y", lambda e: e.dma_start(d[:, :], src_ap), inc="ds", k=16)
            op("y", lambda e: e.wait_ge(P.sems["ds"], cnt["ds"]))
            P.block()

        tap("dxn", xng[:, :].rearrange("p (c t) -> p c t", c=C))
        if PHASE_LIMIT < 2:
            return nc, ctx

        k = 0
        sq2 = {}
        for j in range(J):
            for o in range(C):
                bre, bim = bank(2 * (k % 2)), bank(2 * (k % 2) + 1)
                w = [("sa", sq2.get(k - 2, 0))] if k >= 2 else (
                    [ln0_done, sin_toks[7], sin_toks[15]] if k == 0 else [])
                for a in range(C):
                    op("t", (lambda a=a, o=o, j=j, bre=bre: lambda e: e.matmul(
                        bre, warA[:, a * 1024 + o * 128:a * 1024 + o * 128 + 128],
                        xg(a, j), start=(a == 0), stop=(a == C - 1)))(),
                        waits=(w if a == 0 else []))
                mt = None
                for a in range(C):
                    mt = op("t", (lambda a=a, o=o, j=j, bim=bim: lambda e: e.matmul(
                        bim, warB[:, a * 1024 + o * 128:a * 1024 + o * 128 + 128],
                        xg(a, j), start=(a == 0), stop=(a == C - 1)))(),
                        inc=("sp" if a == C - 1 else None))
                op("s", (lambda bre=bre: lambda e: e.activation(sc1, bre, AF.Square))(),
                   waits=[("sp", mt)])
                s2 = op("s", (lambda bim=bim: lambda e: e.activation(sc2, bim, AF.Square))(),
                        inc="sa")
                sq2[k] = s2
                vt = op("v", lambda e: e.tensor_tensor(sc1, sc1, sc2, ALU.add),
                        waits=[("sa", s2)], inc="sv")
                op("s", (lambda o=o, j=j: lambda e: e.activation(
                    wg(o, j), sc1, AF.Sqrt))(), waits=[("sv", vt)], inc="sa")
                k += 1
        b2a_last_mm = cnt["sp"]
        if PHASE_LIMIT < 3:
            return nc, ctx

        evac = {}
        for j in range(J):
            lastmm = None
            for o in range(C):
                w = [("ds", tok_w0qkv)] if (j == 0 and o == 0) else []
                if j > 0:
                    w += [("sa", evac[(j - 1, o)][0]), ("sv", evac[(j - 1, o)][1])]
                for a in range(C):
                    lastmm = op("t", (lambda a=a, o=o, j=j: lambda e: e.matmul(
                        bank(o), sgx[:, a * 1024 + o * 128:a * 1024 + o * 128 + 128],
                        xg(a, j), start=(a == 0), stop=(a == C - 1)))(),
                        waits=(w if a == 0 else []),
                        inc=("sp" if a == C - 1 else None))
            for o in range(C):
                dst = wsq0 if o % 2 == 0 else wsq1
                wq = [("sp", lastmm)] if o == 0 else []
                if o >= 2:
                    wq.append(("sv", evac[(j, o - 2)][1]))
                at = op("s", (lambda o=o, j=j, dst=dst: lambda e: e.activation(
                    dst, bank(o), AF.Square))(), waits=wq, inc="sa")
                op("v", (lambda o=o, j=j: lambda e: e.tensor_copy(
                    xg(o, j), bank(o)))(), waits=[("sa", at)])
                vt = op("v", (lambda o=o, j=j, dst=dst: lambda e: e.tensor_reduce(
                    nsqpart[:, 8 * o + j:8 * o + j + 1], dst, AX.X, ALU.add))(),
                    inc="sv")
                evac[(j, o)] = (at, vt)
        if PHASE_LIMIT < 4:
            return nc, ctx

        op("v", lambda e: e.tensor_reduce(
            d8[:, :], nsqpart[:, :].rearrange("p (o j) -> p o j", j=8),
            AX.X, ALU.add), waits=[("pl", tok_selinv0)])
        vspacer()
        op("v", lambda e: e.tensor_scalar(d8[:, :], d8[:, :], 1e-24, 0.0,
                                          ALU.max, ALU.add))
        vspacer()
        op("v", lambda e: e.reciprocal(d8[:, :], d8[:, :]))
        vspacer()
        op("v", lambda e: e.tensor_copy(inv2b[:, :], d8[:, :]))
        vspacer()
        for o in range(C):
            op("v", (lambda o=o: lambda e: e.tensor_copy(
                selinv[0:64, 18 * o:18 * o + 1], inv2b[0:64, o:o + 1]))())
            op("v", (lambda o=o: lambda e: e.tensor_copy(
                selinv[64:128, 18 * o + 1:18 * o + 2], inv2b[64:128, o:o + 1]))())
        vspacer(inc="sv")
        if PHASE_LIMIT < 5:
            return nc, ctx

        for h in range(16):
            op("s", (lambda h=h: lambda e: e.activation(
                wbuf[:, 2048 * h:2048 * h + 2048], wbuf[:, 2048 * h:2048 * h + 2048],
                AF.Silu))(), inc="sa")
        tap("dw", xng[:, :].rearrange("p (c t) -> p c t", c=C))
        tap("dgg", wbuf[:, :].rearrange("p (c t) -> p c t", c=C))
        if PHASE_LIMIT < 6:
            return nc, ctx

        tok_w0out = op("g", lambda e: e.dma_start(
            warA[:, :].rearrange("p (a o) -> p a o", a=C),
            w0outT.rearrange("(a p) o -> p a o", p=128)),
            waits=[("sp", b2a_last_mm)], inc="dg", k=16)
        tok_cbt = op("g", lambda e: e.dma_start(
            warB[:, :].rearrange("p (a o) -> p a o", a=C),
            cbT.rearrange("(a p) o -> p a o", p=128)), inc="dg", k=16)
        mmt = {}
        evt_j = {}
        for j in range(J):
            for c in range(C):
                src = wsq0 if c % 2 == 0 else wsq1
                w = [("sp", mmt.get((j, c - 2), mmt.get((j - 1, C - 1), 0)))] \
                    if (c >= 2 or j > 0) else []
                vt = op("v", (lambda src=src, c=c, j=j: lambda e: e.tensor_tensor(
                    src, xg(c, j), xg(c, j), ALU.mult))(), waits=w, inc="sv")
                wm = [("sv", vt)]
                if c == 0 and j > 0:
                    wm.append(("sv", evt_j[j - 1]))
                mmt[(j, c)] = op("t", (lambda src=src, c=c: lambda e: e.matmul(
                    ps[0:16, 0:512], selinv[:, 16 * c:16 * c + 16], src,
                    start=(c == 0), stop=(c == C - 1)))(),
                    waits=wm, inc="sp")
            evt_j[j] = op("v", (lambda j=j: lambda e: e.tensor_scalar(
                hm16[0:16, 512 * j:512 * j + 512], ps[0:16, 0:512],
                temp_s[:, :], 0.0, ALU.mult, ALU.add))(),
                waits=[("sp", mmt[(j, C - 1)])], inc="sv")
        tap("dnsq", nsqpart[:, :], rearr=False)
        tap("dinv", selinv[:, :], rearr=False)
        tap("dlg", hm16[0:16, :], rearr=False)
        if PHASE_LIMIT < 7:
            return nc, ctx

        lgT = Pi_tm
        cp_t = {}
        for k in range(32):
            bA = ps[:, (0 + k % 2) * 512:(0 + k % 2) * 512 + 512].bitcast(BF16)[:, 0:16]
            w = [("sv", cp_t.get(k - 2, evt_j[J - 1] if k < 2 else 0))]
            tp1 = op("t", (lambda bA=bA, k=k: lambda e: e.transpose(
                bA, hm16[0:16, 128 * k:128 * k + 128], identb[0:16, 0:16]))(),
                waits=w, inc="sp")
            cp_t[k] = op("v", (lambda bA=bA, k=k: lambda e: e.tensor_copy(
                lgT[:, 16 * k:16 * k + 16], bA))(), waits=[("sp", tp1)], inc="sv")
        vspacer()
        lg3 = lgT[:, :].rearrange("p (k h) -> p k h", h=16)
        mxbc = bass.AP(m32, 0, [[32, 128], [1, 32], [0, 16]])
        op("v", lambda e: e.tensor_reduce(m32[:, :], lg3, AX.X, ALU.max))
        vspacer()
        svt = op("v", lambda e: e.tensor_tensor(lg3, lg3, mxbc, ALU.subtract),
                 inc="sv")
        ext = op("s", lambda e: e.activation(lgT[:, :], lgT[:, :], AF.Exp),
                 waits=[("sv", svt)], inc="sa")
        op("v", lambda e: e.tensor_reduce(m32[:, :], lg3, AX.X, ALU.add),
           waits=[("sa", ext)])
        vspacer()
        op("v", lambda e: e.reciprocal(m32[:, :], m32[:, :]))
        vspacer()
        pit = op("v", lambda e: e.tensor_tensor(lg3, lg3, mxbc, ALU.mult),
                 inc="sv")
        hm_t = {}
        for k in range(32):
            bB = ps[:, (2 + k % 2) * 512:(2 + k % 2) * 512 + 512].bitcast(BF16)[0:16, 0:128]
            w = [("sv", pit)] if k == 0 else []
            if k >= 2:
                w.append(("sv", hm_t[k - 2]))
            op("t", (lambda k=k: lambda e: e.matmul(
                ps[0:16, 5 * 512:5 * 512 + 1], Pi_tm[:, 16 * k:16 * k + 16],
                ones_bf[:, :], start=(k == 0), stop=(k == 31)))(), waits=w)
            tp2 = op("t", (lambda bB=bB, k=k: lambda e: e.transpose(
                bB, Pi_tm[:, 16 * k:16 * k + 16], identb[:, :]))(), inc="sp")
            hm_t[k] = op("v", (lambda bB=bB, k=k: lambda e: e.tensor_copy(
                hm16[0:16, 128 * k:128 * k + 128], bB))(),
                waits=[("sp", tp2)], inc="sv")
        last_pisum_tok = cnt["sp"]
        op("v", lambda e: e.tensor_copy(pn_f[:, :], ps[0:16, 5 * 512:5 * 512 + 1]),
           waits=[("sp", last_pisum_tok)])
        vspacer()
        op("v", lambda e: e.tensor_scalar(pn_f[:, :], pn_f[:, :], 1e-8, 0.0,
                                          ALU.add, ALU.add))
        vspacer()
        op("v", lambda e: e.reciprocal(pn_f[:, :], pn_f[:, :]))
        vspacer()
        pnt = op("v", lambda e: e.tensor_copy(pn_b[:, :], pn_f[:, :]), inc="sv")
        vspacer(inc="sv")
        for c in range(C):
            bt = op("t", (lambda c=c: lambda e: e.matmul(
                ps[:, 4 * 512 + 64 * c:4 * 512 + 64 * c + 1],
                selb[:, 128 * c:128 * c + 128], pn_b[:, :], start=True, stop=True))(),
                waits=[("sv", cnt["sv"])] if c == 0 else [], inc="sp")
            op("v", (lambda c=c: lambda e: e.tensor_copy(
                pnbc[:, c:c + 1], ps[:, 4 * 512 + 64 * c:4 * 512 + 64 * c + 1]))(),
                waits=[("sp", bt)])
        vspacer(inc="sv")
        tap("dpi", hm16[0:16, :], rearr=False)
        tap("dptm", Pi_tm[:, :], rearr=False)
        if PHASE_LIMIT < 8:
            return nc, ctx

        k = 0
        stt_t = {}
        pl_t = {}
        for c in range(C):
            for j in range(J):
                w = [("sv", stt_t.get(k - 2, 0))] if k >= 2 else (
                    [("sv", hm_t[31])] if k == 0 else [])
                bt = op("t", (lambda c=c, j=j, k=k: lambda e: e.matmul(
                    bank(k % 2), selb[:, 128 * c:128 * c + 128],
                    hm16[0:16, 512 * j:512 * j + 512], start=True, stop=True))(),
                    waits=w, inc="sp")
                wsqb = wsq0 if k % 2 == 0 else wsq1
                pl_t[k] = op("g", (lambda c=c, j=j, wsqb=wsqb: lambda e: e.tensor_tensor(
                    wsqb, xg(c, j), xg(c, j), ALU.mult))(),
                    waits=([("sv", stt_t[k - 2])] if k >= 2
                           else [("sv", hm_t[31])]), inc="pl")
                op("v", (lambda c=c, j=j, k=k, wsqb=wsqb: lambda e: e.scalar_tensor_tensor(
                    junkf, wsqb, 1.0, bank(k % 2), ALU.mult, ALU.mult))(),
                    waits=[("sp", bt), ("pl", pl_t[k])])
                stt_t[k] = op("v", (lambda c=c, j=j: lambda e: e.tensor_reduce(
                    dotpart[:, 8 * c + j:8 * c + j + 1], junkf, AX.X, ALU.add))(),
                    inc="sv")
                k += 1
        op("v", lambda e: e.tensor_reduce(
            d8[:, :], dotpart[:, :].rearrange("p (c j) -> p c j", j=8),
            AX.X, ALU.add))
        vspacer()
        op("v", lambda e: e.tensor_tensor(d8[:, :], d8[:, :], pnbc[:, :], ALU.mult))
        vspacer()
        op("v", lambda e: e.tensor_scalar(d8[:, :], d8[:, :], 1.0, 1.0,
                                          ALU.mult, ALU.add))
        vspacer()
        op("v", lambda e: e.reciprocal(d8[:, :], d8[:, :]))
        vspacer()
        op("v", lambda e: e.tensor_scalar(negattn[:, :], d8[:, :], -1.0, 0.0,
                                          ALU.mult, ALU.add))
        vspacer(inc="sv")
        tap("dna", negattn[:, :], rearr=False)
        if PHASE_LIMIT < 9:
            return nc, ctx

        op("g", g_relay("dg", tok_cbt))
        cnt["dr"] += 1
        tok_R3 = cnt["dr"]
        oe_t = {}
        sg_t = {}
        acc_t = {}
        for j in range(J):
            for c in range(C):
                w = [("sv", stt_t[63])] if (j == 0 and c == 0) else []
                if c >= 2:
                    w.append(("sv", oe_t[(j, c - 2)]))
                elif j > 0:
                    w.append(("sv", oe_t[(j - 1, C - 2 + c)]))
                if j > 0:
                    w.append(("sp", sproj_last[0]))
                bt = op("t", (lambda c=c, j=j: lambda e: e.matmul(
                    bank(c % 2), selb[:, 128 * c:128 * c + 128],
                    hm16[0:16, 512 * j:512 * j + 512], start=True, stop=True))(),
                    waits=w, inc="sp")
                oe_t[(j, c)] = op("v", (lambda c=c, j=j: lambda e: e.scalar_tensor_tensor(
                    pbuf[:, 512 * c:512 * c + 512], xg(c, j), negattn[:, c:c + 1],
                    bank(c % 2), ALU.mult, ALU.mult))(),
                    waits=[("sp", bt)], inc="sv")
            lastmm = None
            for o in range(C):
                w = [("sv", oe_t[(j, C - 1)])] if o == 0 else []
                if o == 0 and j == 0:
                    w.append(("dr", tok_R3))
                if o >= 2:
                    w.append(("sv", sg_t[(j, o - 2)]))
                elif j > 0:
                    w.append(("sv", sg_t[(j - 1, C - 2 + o)]))
                for a in range(C):
                    lastmm = op("t", (lambda a=a, o=o: lambda e: e.matmul(
                        bank(2 + o % 2), warA[:, a * 1024 + o * 128:a * 1024 + o * 128 + 128],
                        pbuf[:, 512 * a:512 * a + 512],
                        start=(a == 0), stop=(a == C - 1)))(),
                        waits=(w if a == 0 else []),
                        inc=("sp" if a == C - 1 else None))
                ww = [("sp", lastmm)]
                if j >= 1:
                    ww.append(("dg", acc_t[j - 1]))
                sg_t[(j, o)] = op("v", (lambda o=o, j=j: lambda e: e.scalar_tensor_tensor(
                    sg_f[:, 512 * o:512 * o + 512], bank(2 + o % 2), bout_s[:, o:o + 1],
                    wg(o, j), ALU.add, ALU.mult))(), waits=ww, inc="sv")
            sproj_last = [lastmm]
            w = [("sv", sg_t[(j, C - 1)])]
            op("g", (lambda j=j: lambda e: e.dma_start(
                sg_f[:, :].rearrange("p (c t) -> p c t", c=C),
                xin_r[:, :, 512 * j:512 * j + 512], accum_op=ALU.add))(),
                waits=w, inc="dg", k=16)
            acc_t[j] = op("g", (lambda j=j: lambda e: e.dma_start(
                x1s_r[:, :, 512 * j:512 * j + 512],
                sg_f[:, :].rearrange("p (c t) -> p c t", c=C)))(),
                inc="dg", k=16)
        tok_x1done = acc_t[J - 1]
        P.block()
        if PHASE_LIMIT < 10:
            return nc, ctx

        tok_xout_init = [0]
        tok_tables = [0]
        tok_w1out = [0]

        def b7_extra():
            op("y", lambda e: e.dma_start(sgx[:, 0:4096], cosvd[:, :]), inc="ds", k=16)
            tok_tables[0] = op("y", lambda e: e.dma_start(sgx[:, 4096:8192],
                                                          sinvd[:, :]), inc="ds", k=16)
            tok_w1out[0] = op("y", lambda e: e.dma_start(
                warA[:, :].rearrange("p (a o) -> p a o", a=C),
                w1out.rearrange("(a p) o -> p a o", p=128)), inc="ds", k=16)
        b6_mm_done = sproj_last[0]
        ln_phase(x1s_r, extra=b7_extra,
                 jwaits=lambda j: [("dg", acc_t[j]), ("sp", b6_mm_done)])

        tap("dxn1", xng[:, :].rearrange("p (c t) -> p c t", c=C))
        if PHASE_LIMIT < 11:
            return nc, ctx

        QSB, KSB, QP, KP = 0, 4096, 8192, 12288
        VTO, AOF, RSC = 16384, 22528, 26624
        qsb = wbuf[:, QSB:QSB + 4096]
        ksb = wbuf[:, KSB:KSB + 4096]
        Qp_t = wbuf[:, QP:QP + 4096]
        Kp_t = wbuf[:, KP:KP + 4096]
        aof = wbuf[:, AOF:AOF + 4096]
        rsc1 = wbuf[:, RSC:RSC + 1024].bitcast(F32)
        rsc2 = wbuf[:, RSC + 1024:RSC + 2048].bitcast(F32)
        cosv_sb = sgx[:, 0:4096]
        sinv_sb = sgx[:, 4096:8192]
        wsl = xbf
        pA = pbuf[:, 0:2048]
        pB = pbuf[:, 2048:4096]

        def vto_ap(tt):
            return bass.AP(wbuf, VTO + 192 * tt, [[32768, 128], [128, 2], [1, 64]])

        def vto_sl(tt, h):
            lo = VTO + 192 * tt + 64 * h
            return wbuf[:, lo:lo + 128]

        aof_dma_tok = [0]
        tok_vto_ones = [0]
        tok_R7 = [0]
        prev_vv31 = [0]
        for p in range(NPAIR):
            if p == 0:
                op("g", g_relay("dg", cnt["dg"]))
                cnt["dr"] += 1
                tok_R7[0] = cnt["dr"]
                tok_vto_ones[0] = op("g", lambda e: e.memset(
                    wbuf[:, VTO:VTO + 6144], 1.0), inc="pl")
            dws = None
            for qi, off in ((0, 128 * p), (1, D + 128 * p), (2, 2 * D + 128 * p)):
                wd = [("sv", prev_vv31[0])] if (qi == 0 and p > 0) else []
                dws = op("g", (lambda qi=qi, off=off: lambda e: e.dma_start(
                    wsl[:, 1024 * qi:1024 * qi + 1024].rearrange("p (a o) -> p a o", a=C),
                    w1qkv.rearrange("(a p) d -> p a d", p=128)[:, :, off:off + 128]))(),
                    waits=wd, inc="dg", k=16)
            ev_t = {}
            for qi in range(2):
                dst = qsb if qi == 0 else ksb
                for j in range(J):
                    kk = qi * J + j
                    w = [("dg", dws)] if kk == 0 else []
                    if kk >= 2:
                        w.append(("sv", ev_t[kk - 2]))
                    lm = None
                    for a in range(C):
                        lm = op("t", (lambda qi=qi, a=a, j=j, kk=kk: lambda e: e.matmul(
                            bank(kk % 2), wsl[:, 1024 * qi + a * 128:1024 * qi + a * 128 + 128],
                            xg(a, j), start=(a == 0), stop=(a == C - 1)))(),
                            waits=(w if a == 0 else []),
                            inc=("sp" if a == C - 1 else None))
                    ev_t[kk] = op("v", (lambda dst=dst, j=j, kk=kk: lambda e: e.tensor_copy(
                        dst[:, 512 * j:512 * j + 512], bank(kk % 2)))(),
                        waits=[("sp", lm)], inc="sv")
            vv_t = {}
            for tt in range(32):
                w = []
                if tt == 0:
                    w = [("pl", tok_vto_ones[0]), ("dg", dws)]
                if tt >= 2:
                    w.append(("sv", vv_t[tt - 2]))
                lm = None
                for a in range(C):
                    lm = op("t", (lambda a=a, tt=tt: lambda e: e.matmul(
                        bank(2 + tt % 2, 128),
                        xng[:, 4096 * a + 128 * tt:4096 * a + 128 * tt + 128],
                        wsl[:, 2048 + a * 128:2048 + a * 128 + 128],
                        start=(a == 0), stop=(a == C - 1)))(),
                        waits=(w if a == 0 else []),
                        inc=("sp" if a == C - 1 else None))
                vv_t[tt] = op("v", (lambda tt=tt: lambda e: e.tensor_copy(
                    vto_ap(tt), bank(2 + tt % 2, 128)))(),
                    waits=[("sp", lm)], inc="sv")
            prev_vv31[0] = vv_t[31]
            rot_t = {}
            for qi in range(2):
                src = qsb if qi == 0 else ksb
                dst = Qp_t if qi == 0 else Kp_t
                for j in range(J):
                    kk = qi * J + j
                    w = [("sv", ev_t[kk])]
                    if p == 0 and kk == 0:
                        w.append(("ds", tok_tables[0]))
                    if kk >= 2:
                        w.append(("sv", rot_t[kk - 2]))
                    pm = op("t", (lambda src=src, j=j, kk=kk: lambda e: e.matmul(
                        bank(4 + kk % 2), permb[:, :], src[:, 512 * j:512 * j + 512],
                        start=True, stop=True))(), waits=w, inc="sp")
                    op("v", (lambda src=src, j=j: lambda e: e.tensor_tensor(
                        rsc1, src[:, 512 * j:512 * j + 512],
                        cosv_sb[:, 512 * j:512 * j + 512], ALU.mult))(),
                        waits=[("sv", ev_t[kk])])
                    op("v", (lambda j=j, kk=kk: lambda e: e.tensor_tensor(
                        rsc2, bank(4 + kk % 2), sinv_sb[:, 512 * j:512 * j + 512],
                        ALU.mult))(), waits=[("sp", pm)])
                    rot_t[kk] = op("v", (lambda dst=dst, j=j: lambda e: e.tensor_tensor(
                        dst[:, 512 * j:512 * j + 512], rsc1, rsc2, ALU.add))(),
                        inc="sv")
            if DEBUG and p == 0:
                op("y", lambda e: e.dma_start(dbg["dqp"][:, :], Qp_t[:, :]), inc="ds", k=16)
                op("y", lambda e: e.dma_start(dbg["dkp"][:, :], Kp_t[:, :]), inc="ds", k=16)
                op("y", lambda e: e.dma_start(dbg["dvt"][:, :], wbuf[:, VTO:VTO + 6144]),
                   inc="ds", k=16)
                op("y", lambda e: e.wait_ge(P.sems["ds"], cnt["ds"]))
                P.block()

            units = [(qt, kc) for qt in range(8) for kc in range(4 * qt + 4)]
            NU = len(units)
            exp_u = {}
            pv_u = {}
            evac_q = {}

            def PQ(ui):
                return pbuf[:, 1024 * (ui % 4):1024 * (ui % 4) + 1024]

            def emit_qk(ui):
                qt, kc = units[ui]
                c0 = max(0, kc * 128 - qt * 512)
                diag = 0 <= kc * 128 - qt * 512 < 512
                sbb = 2 * (ui % 2)
                w = []
                if ui >= 2:
                    w.append(("sa", exp_u[ui - 2]))
                if kc == 0:
                    w.append(("sv", rot_t[qt]))
                if qt == kc // 4:
                    w.append(("sv", rot_t[8 + kc // 4]))
                qk_tok = None
                for h in range(2):
                    ww = w if h == 0 else []
                    stop = not diag
                    qk_tok = op("t", (lambda h=h, c0=c0, kc=kc, qt=qt, sbb=sbb, stop=stop:
                                      lambda e: e.matmul(
                        ps[:, 512 * sbb + 512 * h + c0:512 * sbb + 512 * h + 512],
                        Kp_t[64 * h:64 * h + 64, 128 * kc:128 * kc + 128],
                        Qp_t[64 * h:64 * h + 64, 512 * qt + c0:512 * qt + 512],
                        start=True, stop=stop, tile_position=(64 * h, 0)))(),
                        waits=ww,
                        inc=("sp" if (h == 1 and not diag) else None))
                if diag:
                    for h in range(2):
                        qk_tok = op("t", (lambda h=h, c0=c0, sbb=sbb: lambda e: e.matmul(
                            ps[:, 512 * sbb + 512 * h + c0:512 * sbb + 512 * h + c0 + 128],
                            identb[:, :], maskb[:, :], start=False, stop=True))(),
                            inc=("sp" if h == 1 else None))
                return qk_tok

            qk_toks = {0: emit_qk(0), 1: emit_qk(1)}
            for ui, (qt, kc) in enumerate(units):
                c0 = max(0, kc * 128 - qt * 512)
                sbb = 2 * (ui % 2)
                pbb = PQ(ui)
                aob = 4 + 2 * (qt % 2)
                first = kc == 0
                last = kc == 4 * qt + 3
                w = [("sp", qk_toks[ui])]
                if ui >= 4:
                    w.append(("sp", pv_u[ui - 4]))
                if c0 == 0:
                    exp_u[ui] = op("s", (lambda sbb=sbb, pbb=pbb: lambda e: e.activation(
                        pbb[:, 0:1024], ps[:, 512 * sbb:512 * sbb + 1024], AF.Exp,
                        bias=f32(-8.0), scale=f32(0.125)))(), waits=w, inc="sa")
                else:
                    op("s", (lambda sbb=sbb, pbb=pbb, c0=c0: lambda e: e.activation(
                        pbb[:, c0:512], ps[:, 512 * sbb + c0:512 * sbb + 512], AF.Exp,
                        bias=f32(-8.0), scale=f32(0.125)))(), waits=w)
                    exp_u[ui] = op("s", (lambda sbb=sbb, pbb=pbb, c0=c0: lambda e: e.activation(
                        pbb[:, 512 + c0:1024], ps[:, 512 * sbb + 512 + c0:512 * sbb + 1024],
                        AF.Exp, bias=f32(-8.0), scale=f32(0.125)))(), inc="sa")
                for h in range(2):
                    w = [("sa", exp_u[ui])] if h == 0 else []
                    if h == 0 and qt == kc // 4:
                        w.append(("sv", vv_t[kc]))
                    if h == 0 and first and qt >= 2:
                        w.append(("sv", evac_q[qt - 2]))
                    pv_u[ui] = op("t", (lambda h=h, kc=kc, qt=qt, c0=c0, pbb=pbb, aob=aob,
                                        first=first, last=last: lambda e: e.matmul(
                        ps[:, 512 * (aob + h) + c0:512 * (aob + h) + 512],
                        vto_sl(kc, h), pbb[:, 512 * h + c0:512 * h + 512],
                        start=first, stop=last))(),
                        waits=w, inc=("sp" if h == 1 else None))
                if ui + 2 < NU:
                    qk_toks[ui + 2] = emit_qk(ui + 2)
                if last:
                    et = None
                    for h in range(2):
                        aorow = 0 if h == 0 else 64
                        denrow = 64 if h == 0 else 0
                        w = [("sp", pv_u[ui])]
                        if p > 0 and qt == 0 and h == 0:
                            w.append(("ds", aof_dma_tok[0]))
                        op("v", (lambda h=h, aob=aob, denrow=denrow: lambda e: e.reciprocal(
                            rbcsb[:, :],
                            ps[denrow:denrow + 64, 512 * (aob + h):512 * (aob + h) + 512]))(),
                            waits=w)
                        et = op("v", (lambda h=h, qt=qt, aob=aob, aorow=aorow:
                                      lambda e: e.tensor_tensor(
                            aof[64 * h:64 * h + 64, 512 * qt:512 * qt + 512],
                            ps[aorow:aorow + 64, 512 * (aob + h):512 * (aob + h) + 512],
                            rbcsb[:, :], ALU.mult))(), inc="sv")
                    evac_q[qt] = et
            aof_dma_tok[0] = op("y", (lambda p=p: lambda e: e.dma_start(
                aos[128 * p:128 * p + 128, :], aof[:, :]))(),
                waits=[("sv", evac_q[7])], inc="ds", k=16)
        P.block()
        if PHASE_LIMIT < 12:
            return nc, ctx

        fsc = xbf[:, 0:1024].bitcast(F32)
        g1_t = {}
        ffn_vt = {}
        sg1_t = {}
        acc1_t = {}
        ldj = {}
        sproj_done = {}
        for j in range(J):
            w = [] if j == 0 else [("sp", sproj_done[j - 1])]
            ld0 = op("y", (lambda j=j: lambda e: e.dma_start(
                pbuf[:, :].rearrange("p (c t) -> p c t", c=C),
                aos_r[:, :, 512 * j:512 * j + 512]))(), waits=w, inc="ds", k=16)
            ldj[j] = ld0
            for o in range(C):
                w = []
                if j == 0 and o == 0:
                    w += [("dr", tok_R3), ("ds", ldj[0] if 0 in ldj else 1)]
                if o >= 2:
                    w.append(("sv", ffn_vt[(j, o - 2)]))
                elif j > 0:
                    w.append(("sv", ffn_vt[(j - 1, C - 2 + o)]))
                lm = None
                for a in range(C):
                    lm = op("t", (lambda a=a, o=o, j=j: lambda e: e.matmul(
                        bank(o % 2), warB[:, a * 1024 + o * 128:a * 1024 + o * 128 + 128],
                        xg(a, j), start=(a == 0), stop=(a == C - 1)))(),
                        waits=(w if a == 0 else []),
                        inc=("sp" if a == C - 1 else None))
                wv = [("sp", lm)]
                prev = (j, o - 1) if o > 0 else ((j - 1, C - 1) if j > 0 else None)
                if prev is not None:
                    wv.append(("sa", g1_t[prev]))
                ffn_vt[(j, o)] = op("v", (lambda o=o: lambda e: e.tensor_scalar(
                    fsc, bank(o % 2), scb_s[:, o:o + 1], bias1_s[:, o:o + 1],
                    ALU.mult, ALU.add))(), waits=wv, inc="sv")
                g1_t[(j, o)] = op("s", (lambda o=o, j=j: lambda e: e.activation(
                    wg(o, j), fsc, AF.Silu))(),
                    waits=[("sv", ffn_vt[(j, o)])], inc="sa")
            lastmm = None
            for o in range(C):
                w = [("ds", ldj[j])] if o == 0 else []
                if o == 0 and j == 0:
                    w.append(("ds", tok_w1out[0]))
                if o >= 2:
                    w.append(("sv", sg1_t[(j, o - 2)]))
                elif j > 0:
                    w.append(("sv", sg1_t[(j - 1, C - 2 + o)]))
                for a in range(C):
                    lastmm = op("t", (lambda a=a, o=o: lambda e: e.matmul(
                        bank(2 + o % 2), warA[:, a * 1024 + o * 128:a * 1024 + o * 128 + 128],
                        pbuf[:, 512 * a:512 * a + 512],
                        start=(a == 0), stop=(a == C - 1)))(),
                        waits=(w if a == 0 else []),
                        inc=("sp" if a == C - 1 else None))
                ww = [("sp", lastmm), ("sa", g1_t[(j, o)])]
                if j >= 1:
                    ww.append(("dg", acc1_t[j - 1]))
                sg1_t[(j, o)] = op("v", (lambda o=o, j=j: lambda e: e.scalar_tensor_tensor(
                    sg_f[:, 512 * o:512 * o + 512], bank(2 + o % 2), 1.0,
                    wg(o, j), ALU.mult, ALU.mult))(), waits=ww, inc="sv")
            sproj_done[j] = lastmm
            w = [("sv", sg1_t[(j, C - 1)])]
            op("g", (lambda j=j: lambda e: e.dma_start(
                sg_f[:, :].rearrange("p (c t) -> p c t", c=C),
                x1s_r[:, :, 512 * j:512 * j + 512], accum_op=ALU.add))(),
                waits=w, inc="dg", k=16)
            acc1_t[j] = op("g", (lambda j=j: lambda e: e.dma_start(
                xout_r[:, :, 512 * j:512 * j + 512],
                sg_f[:, :].rearrange("p (c t) -> p c t", c=C)))(),
                inc="dg", k=16)
        op("g", lambda e: e.wait_ge(P.sems["dg"], cnt["dg"]))
        P.block()

    return nc, None


def _host_prep(inputs):
    L0x = np.asarray(inputs["L0_attn_x"], dtype=np.float32)
    B = L0x.shape[0]
    pos = np.arange(T, dtype=np.float32)[:, None]
    ii = np.arange(0, 64, 2, dtype=np.float32)[None, :]
    theta = (pos / np.power(f32(10000.0), ii / f32(64.0))).astype(np.float32)
    cost = np.cos(theta).astype(np.float32)
    sint = np.sin(theta).astype(np.float32)
    dd = (np.arange(128) % 64) % 32
    cosv = np.ascontiguousarray(cost[:, dd].T)
    sinv = np.ascontiguousarray(sint[:, dd].T)
    perm = np.zeros((128, 128), dtype=np.float32)
    for jj in range(128):
        base = (jj // 64) * 64
        d = jj % 64
        if d < 32:
            perm[base + d + 32, jj] = -1.0
        else:
            perm[base + d - 32, jj] = 1.0
    ident = np.eye(128, dtype=np.float32)
    mask = np.where(np.arange(128)[:, None] > np.arange(128)[None, :],
                    np.float32(-1e9), np.float32(0.0)).astype(np.float32)
    sel = np.zeros((16, D), dtype=np.float32)
    for ff in range(D):
        sel[2 * (ff // 128) + (ff % 128) // 64, ff] = 1.0

    common = {
        "w0qkvT": np.ascontiguousarray(np.asarray(inputs["L0_attn_Wqkv"], np.float32).T.astype(ml_dtypes.bfloat16)),
        "w0outT": np.ascontiguousarray(np.asarray(inputs["L0_attn_Wout"], np.float32).T),
        "tempd": np.ascontiguousarray(np.asarray(inputs["L0_attn_temp"], np.float32)),
        "boutd": np.ascontiguousarray(
            np.asarray(inputs["L0_attn_bout"], np.float32).reshape(8, 128).T),
        "th0": np.ascontiguousarray(np.asarray(inputs["L0_ffn_theta"], np.float32)),
        "thn0": np.ascontiguousarray(np.asarray(inputs["L0_ffn_theta_noise"], np.float32)),
        "ph0": np.ascontiguousarray(np.asarray(inputs["L0_ffn_phi"], np.float32)),
        "phn0": np.ascontiguousarray(np.asarray(inputs["L0_ffn_phi_noise"], np.float32)),
        "w1qkv": np.ascontiguousarray(np.asarray(inputs["L1_attn_w_qkv"], np.float32)),
        "w1out": np.ascontiguousarray(np.asarray(inputs["L1_attn_w_out"], np.float32).astype(ml_dtypes.bfloat16)),
        "cbT": np.ascontiguousarray(np.asarray(inputs["L1_ffn_CB"], np.float32).T),
        "scbd": np.ascontiguousarray(
            np.asarray(inputs["L1_ffn_SCB"], np.float32).reshape(8, 128).T),
        "bias1d": np.ascontiguousarray(
            np.asarray(inputs["L1_ffn_bias"], np.float32).reshape(8, 128).T),
        "cosvd": cosv.astype(ml_dtypes.bfloat16), "sinvd": sinv.astype(ml_dtypes.bfloat16), "permd": perm, "identd": ident,
        "maskd": mask, "seld": sel,
    }
    in_maps = []
    for b in range(B):
        m = dict(common)
        m["xin"] = np.ascontiguousarray(L0x[b].T)
        in_maps.append(m)
    return in_maps


def kernel(**inputs):
    in_maps = _host_prep(inputs)
    nc, _ = build()
    res = run_bass_kernel_spmd(nc, in_maps, core_ids=list(range(8)))
    out = np.stack([np.ascontiguousarray(r["xout"].T) for r in res.results], axis=0)
    return out.astype(np.float32)
